# version 06
# baseline (speedup 1.0000x reference)
import numpy as np
import ml_dtypes
import concourse.bass as bass
import concourse.mybir as mybir
from concourse.bass_utils import run_bass_kernel_spmd
from contextlib import ExitStack

F32 = mybir.dt.float32
BF16 = mybir.dt.bfloat16
I32 = mybir.dt.int32
AF = mybir.ActivationFunctionType
ALU = mybir.AluOpType
BF = ml_dtypes.bfloat16

D = 4096
T = 1024
NC = 8
CH = 128
H = 32
HD = 128
KV = 8
ND = D // 128
LN_EPS = 1e-5
DEBUG_SPILL = False
STOP_AFTER = "all"
RMS_EPS = 1e-5
NEG = -1e30
PI = float(np.pi)

ENGS = ("sync", "gpsimd", "tensor", "vector", "scalar")
ENG_SEM = {"tensor": "sT", "vector": "sV", "scalar": "sA", "gpsimd": "sP"}
DMA_STREAMS = ("cst", "cp", "cr", "cn", "c5", "xt0", "xt1", "wb0", "wb1", "wc0", "wc1",
               "qb", "cl", "gr", "eg", "gm", "sg0", "sg1", "x2o", "qo", "go", "co", "ro", "oo", "cb", "fq", "fm", "fo", "w50", "w51")
ALL_SEMS = DMA_STREAMS + ("sT", "sV", "sA", "sP", "sC")


class Sched:
    def __init__(self):
        self.items = {k: [] for k in ENGS}
        self.counts = {s: 0 for s in ALL_SEMS}
        self.last_wait = {k: {} for k in ENGS}
        self.sems = {}

    def wait(self, eng, tok):
        if tok is None:
            return
        sem, val = tok
        if val <= 0 or val <= self.last_wait[eng].get(sem, 0):
            return
        self.last_wait[eng][sem] = val
        self.items[eng].append(["wait", (sem, val), None])

    def waits(self, eng, toks):
        for t in toks:
            self.wait(eng, t)

    def dma(self, eng, out_ap, in_ap, deps=(), slow=False, stream="cst"):
        self.waits(eng, deps)
        sem = stream
        self.counts[sem] += 16
        self.items[eng].append(["dma", lambda e, o=out_ap, i=in_ap, s=slow: e.dma_start(
            out=o, in_=i, allow_slow_non_contiguous=s), sem])
        return (sem, self.counts[sem])

    def op(self, eng, fn, deps=(), inc=False):
        self.waits(eng, deps)
        if inc:
            sem = ENG_SEM[eng]
            self.counts[sem] += 1
            self.items[eng].append(["op", fn, sem])
            return (sem, self.counts[sem])
        self.items[eng].append(["op", fn, None])
        return None

    def mark(self, eng):
        sem = ENG_SEM.get(eng)
        for item in reversed(self.items[eng]):
            if item[0] == "op":
                if item[2] is None:
                    self.counts[sem] += 1
                    item[2] = sem
                return (item[2], self.counts[item[2]])
            if item[0] == "dma":
                return (item[2], self.counts[item[2]])
        return None

    def barrier(self):
        toks = [self.mark(e) for e in ("tensor", "vector", "scalar", "gpsimd")]
        toks += [(s, self.counts[s]) for s in DMA_STREAMS if self.counts[s] > 0]
        for eng in ENGS:
            self.waits(eng, toks)

    def replay(self, block, sems):
        self.sems = sems
        for eng in ENGS:
            items = self.items[eng]

            def make(items):
                def body(e):
                    for kind, fn, sem in items:
                        if kind == "wait":
                            s, v = fn
                            e.wait_ge(self.sems[s], v)
                        else:
                            inst = fn(e)
                            if sem is not None:
                                inst.then_inc(self.sems[sem], 16 if kind == "dma" else 1)
                return body

            getattr(block, eng)(make(items))


def build_program(replica_groups):
    nc = bass.Bass("TRN2", target_bir_lowering=False, debug=False)
    S = Sched()
    st = ExitStack()

    def SB(name, shape, dtype):
        return st.enter_context(nc.sbuf_tensor(name, shape, dtype))

    EI = "ExternalInput"
    xT = nc.dram_tensor("xT", [D, T], F32, kind=EI).ap()
    wqt = nc.dram_tensor("wqt", [D, D], F32, kind=EI).ap()
    wkt = nc.dram_tensor("wkt", [D, KV * HD], F32, kind=EI).ap()
    wvt = nc.dram_tensor("wvt", [D, KV * HD], F32, kind=EI).ap()
    wgt = nc.dram_tensor("wgt", [D, D], F32, kind=EI).ap()
    wot = nc.dram_tensor("wot", [D, D], F32, kind=EI).ap()
    wint = nc.dram_tensor("wint", [D, H], F32, kind=EI).ap()
    dtb = nc.dram_tensor("dtb", [H, 1], F32, kind=EI).ap()
    alog = nc.dram_tensor("alog", [H, 1], F32, kind=EI).ap()
    gnw = nc.dram_tensor("gnw", [HD, 1], F32, kind=EI).ap()
    wq_qs = nc.dram_tensor("wq_qs", [4, 128, D], I32, kind=EI).ap()
    wq_ms = nc.dram_tensor("wq_ms", [4, 128, D], I32, kind=EI).ap()
    wq_fs = nc.dram_tensor("wq_fs", [128, 4 * 32], F32, kind=EI).ap()
    wq_zs = nc.dram_tensor("wq_zs", [128, 4 * 32], F32, kind=EI).ap()
    wq_s2s = nc.dram_tensor("wq_s2s", [128, 4], F32, kind=EI).ap()
    wq_b = nc.dram_tensor("wq_b", [1, D], F32, kind=EI).ap()
    cosd_d = nc.dram_tensor("cos_d", [128, T], BF16, kind=EI).ap()
    sind_d = nc.dram_tensor("sin_d", [128, T], BF16, kind=EI).ap()
    ccol_d = nc.dram_tensor("ccol", [128, 1], F32, kind=EI).ap()
    eye_bf = nc.dram_tensor("eye_bf", [128, 128], BF16, kind=EI).ap()
    eye_f32 = nc.dram_tensor("eye_f32", [128, 32], F32, kind=EI).ap()
    sl_tri = nc.dram_tensor("sl_tri", [128, 128], BF16, kind=EI).ap()
    negid_d = nc.dram_tensor("negid", [128, 128], BF16, kind=EI).ap()
    negblock = nc.dram_tensor("negblock", [32, D], F32, kind=EI).ap()
    resetm = nc.dram_tensor("resetm", [H, T], F32, kind=EI).ap()
    onesbf = nc.dram_tensor("onesbf", [128, 128], BF16, kind=EI).ap()
    constv = nc.dram_tensor("constv", [128, 3], F32, kind=EI).ap()

    out = nc.dram_tensor("out", [D, T], F32, kind="ExternalOutput").ap()

    q_spill = nc.dram_tensor("q_spill", [H * HD, T], BF16).ap()
    sg_spill = nc.dram_tensor("sg_spill", [H * HD, T], BF16).ap()
    cs_spill = nc.dram_tensor("cs_spill", [NC, 128, H * HD], BF16).ap()
    x2_hbm = nc.dram_tensor("x2_hbm", [D, T], F32).ap()
    gflat_d = nc.dram_tensor("gflat_d", [1, NC * H * CH], F32).ap()
    egflat_d = nc.dram_tensor("egflat_d", [1, NC * H * CH], BF16).ap()
    dbg_yn = nc.dram_tensor("dbg_yn", [128, H * T], BF16).ap()
    dbg_dc = nc.dram_tensor("dbg_dc", [128, H * HD], BF16).ap()
    dbg_ap = nc.dram_tensor("dbg_ap", [128, H * HD], BF16).ap()
    dbg_yi = nc.dram_tensor("dbg_yi", [128, H * HD], BF16).ap()
    gl_dram = nc.dram_tensor("gl_dram", [H, NC], F32).ap()
    cd_dram = nc.dram_tensor("cd_dram", [H, NC], F32).ap()
    cc_in = nc.dram_tensor("cc_in", [H * HD, HD], F32)
    cc_out = nc.dram_tensor("cc_out", [2 * H * HD, HD], F32)
    wdq_in = nc.dram_tensor("wdq_in", [4 * 128, D], BF16)
    wdq_out = nc.dram_tensor("wdq_out", [D, D], BF16)

    big0 = SB("big0", [128, 32768], BF16)
    big1 = SB("big1", [128, 8192], BF16)
    big2 = SB("big2", [128, 8192], BF16)
    big3 = SB("big3", [128, 8192], BF16)
    big5 = SB("big5", [128, 8192], BF16)
    big6 = SB("big6", [128, 4096], BF16)
    big7 = SB("big7", [128, 4096], BF16)
    big8 = SB("big8", [128, 8192], BF16)
    wbuf = [SB(f"wbuf{i}", [128, 4096], BF16) for i in range(2)]
    ebuf = [SB(f"ebuf{i}", [128, 2048], F32) for i in range(2)]
    xtile = [SB(f"xtile{i}", [128, T], F32) for i in range(2)]
    ET = SB("ET", [128, NC * H], F32)
    fv = SB("fv", [128, NC * H], F32)
    cdb = SB("cdb", [128, NC * H], F32)
    identb = SB("identb", [128, 128], BF16)

    identf = SB("identf", [128, 32], F32)
    slT = SB("slT", [128, 128], BF16)
    negidS = SB("negidS", [128, 128], BF16)
    onesb = SB("onesb", [128, 128], BF16)
    smallc = SB("smallc", [128, 16], F32)
    rT = smallc[:, 8:16]

    xc1 = big0.ap()
    ynT = big0.ap()
    kT = big1.ap()
    rbr = big1.ap().bitcast(F32)[:, 0:1024]
    vnat = big2.ap()
    wbuf2 = [big3.ap()[:, 0:4096], big3.ap()[:, 4096:8192]]
    kdec = big3.ap()[:, 0:4096]
    decayT = big3.ap()[:, 0:4096]
    apT = big3.ap()[:, 4096:8192]
    tabf = big5.ap().bitcast(F32)
    cosT = tabf[:, 0:512]
    sinT = tabf[:, 512:1024]
    cosD = big5.ap()[:, 2048:3072]
    sinD = big5.ap()[:, 3072:4096]
    resetS = tabf[0:H, 2048:3072]
    state = big5.ap().bitcast(F32)
    state_bf = big6.ap()
    egrow = big7.ap()[0:1, 2048:4096]
    glhs = big7.ap().bitcast(F32)[0:33, 0:1024]
    onesr = glhs[0:1, 0:128]
    dtp = big8.ap().bitcast(F32)
    dtnat = dtp[0:H, 0:1024]
    gnat = dtp[0:H, 1024:2048]
    Gnat = dtp[0:H, 2048:3072]
    egnat = dtp[0:H, 3072:4096]
    grhs = big8.ap().bitcast(F32)[0:33, 0:4096]
    sgbuf = big8.ap()[:, 0:4096]
    f_all = big1.ap().bitcast(F32)[:, 1024:1324]
    qbuf = wbuf[0].ap()
    csbuf = wbuf[1].ap()

    pA = st.enter_context(nc.psum_tensor("pA", [128, 2048], F32))
    pB = st.enter_context(nc.psum_tensor("pB", [128, 2048], F32))
    pAb = pA.ap().bitcast(BF16)
    pBb = pB.ap().bitcast(BF16)

    nc.const_aps.register(F32, LN_EPS, smallc[:, 5:6])
    nc.const_aps.register(F32, 1.0, smallc[:, 6:7])
    nc.const_aps.register(F32, -PI, smallc[:, 7:8])
    sems = {n: st.enter_context(nc.semaphore(n)) for n in ALL_SEMS}

    def finish():
        with st, nc.Block() as block:
            S.replay(block, sems)

    xTt = xT.rearrange("(n p) t -> n p t", p=128)
    x2t = x2_hbm.rearrange("(n p) t -> n p t", p=128)
    outr = out.rearrange("(n p) t -> n p t", p=128)

    def colslab(w, ot):
        return w.rearrange("(a p) o -> p a o", p=128)[:, :, ot * 128:(ot + 1) * 128]

    def slab3(buf):
        return buf.rearrange("p (a o) -> p a o", a=32)

    def xc1s(i, th=None):
        base = xc1[:, i * T:(i + 1) * T]
        if th is None:
            return base
        return base[:, th * 512:(th + 1) * 512]

    def ynTv(h, lo, hi):
        return ynT.rearrange("p (h t) -> p h t", h=H)[:, h, lo:hi]

    c_eye = S.dma("sync", identb[:], eye_bf)
    c_eyef = S.dma("sync", identf[:], eye_f32)
    c_sl = S.dma("sync", slT[:], sl_tri)
    c_negid = S.dma("sync", negidS[:], negid_d)
    c_ones = S.dma("sync", onesb[:], onesbf)
    gh1 = S.op("gpsimd", lambda e: e.memset(glhs[0:1, :], 1.0), inc=True)
    c_onesr = gh1
    c_cv = S.dma("sync", smallc[:, 5:8], constv)
    c_dtb = S.dma("sync", smallc[0:H, 0:1], dtb)
    c_alog = S.dma("sync", smallc[0:H, 1:2], alog)
    c_gnw = S.dma("sync", smallc[:, 2:3], gnw)
    c_ccol = S.dma("sync", smallc[:, 3:4], ccol_d)
    cst_all = ("cst", S.counts["cst"])
    for nm in ("c_eye", "c_eyef", "c_sl", "c_negid", "c_ones", "c_dtb", "c_alog", "c_gnw",
               "c_ccol", "c_cv", "c_ivf"):
        pass
    c_eye = c_eyef = c_sl = c_negid = c_ones = c_dtb = c_alog = c_gnw = c_ccol = c_cv = cst_all
    t_expA = S.op("scalar", lambda e: e.activation(out=smallc[0:H, 4:5], in_=smallc[0:H, 1:2], func=AF.Exp),
                  deps=[c_alog], inc=True)

    xt_cons = [None, None]
    mm = None
    for i in range(ND):
        b = i % 2
        xl = S.dma("sync", xtile[b][:], xTt[i], deps=(xt_cons[b] or []), stream="xt%d" % b)
        ebb = ebuf[b][:].bitcast(BF16)
        tv = S.op("vector", lambda e, b=b, ebb=ebb: e.tensor_copy(out=ebb[:, 0:T], in_=xtile[b][:]),
                  deps=[xl], inc=True)
        ts_ = S.op("scalar", lambda e, b=b, ebb=ebb: e.activation(out=ebb[:, T:2 * T], in_=xtile[b][:],
                                                                  func=AF.Square), deps=[xl], inc=True)
        xt_cons[b] = [tv, ts_]
        for th in range(2):
            S.op("tensor", lambda e, ebb=ebb, th=th, i=i: e.matmul(
                pA[0:1, th * 512:(th + 1) * 512], onesb[:, 0:1], ebb[:, th * 512:(th + 1) * 512],
                start=(i == 0), stop=(i == ND - 1)), deps=[tv, c_ones])
            mm = S.op("tensor", lambda e, ebb=ebb, th=th, i=i: e.matmul(
                pA[0:1, 1024 + th * 512:1024 + (th + 1) * 512], onesb[:, 0:1],
                ebb[:, 1024 + th * 512:1024 + (th + 1) * 512],
                start=(i == 0), stop=(i == ND - 1)), deps=[ts_], inc=(i == ND - 1 and th == 1))
    stats_done = mm

    murow = ebuf[1][0:1, 0:1024]
    rrow = ebuf[1][0:1, 1024:2048]
    v1row = ebuf[0][0:1, 0:1024]
    v2row = ebuf[0][0:1, 1024:2048]
    t1 = S.op("vector", lambda e: e.tensor_scalar(out=murow, in0=pA[0:1, 0:1024], scalar1=1.0 / D,
                                                  scalar2=None, op0=ALU.mult), deps=[stats_done], inc=True)
    t2 = S.op("vector", lambda e: e.tensor_scalar(out=v1row, in0=pA[0:1, 1024:2048], scalar1=1.0 / D,
                                                  scalar2=None, op0=ALU.mult), deps=[stats_done], inc=True)
    t3 = S.op("vector", lambda e: e.scalar_tensor_tensor(out=v2row, in0=murow, scalar=-1.0, in1=murow,
                                                         op0=ALU.mult, op1=ALU.mult), deps=[t1], inc=True)
    t4 = S.op("vector", lambda e: e.tensor_tensor(out=v1row, in0=v1row, in1=v2row, op=ALU.add),
              deps=[t2, t3], inc=True)
    t5 = S.op("scalar", lambda e: e.activation(out=v2row, in_=v1row, func=AF.Ln, bias=LN_EPS),
              deps=[t4, c_cv], inc=True)
    t6 = S.op("scalar", lambda e: e.activation(out=rrow, in_=v2row, func=AF.Exp, scale=-0.5),
              deps=[t5], inc=True)
    mo = None
    for th in range(2):
        mo = S.op("tensor", lambda e, th=th: e.matmul(pB[:, th * 512:(th + 1) * 512], onesr,
                                                      murow[:, th * 512:(th + 1) * 512], start=True, stop=True),
                  deps=[t1, c_onesr], inc=(th == 1))
    mub_done = mo
    ro = None
    for th in range(2):
        ro = S.op("tensor", lambda e, th=th: e.matmul(pB[:, 1024 + th * 512:1024 + (th + 1) * 512],
                                                      onesr, rrow[:, th * 512:(th + 1) * 512],
                                                      start=True, stop=True), deps=[t6], inc=(th == 1))
    rb_done = ro

    c_cos = S.dma("sync", cosD, cosd_d, deps=[], stream="cp")
    c_sin = S.dma("sync", sinD, sind_d, deps=[], stream="cp")
    xc1_done = None
    for i in range(ND):
        b = i % 2
        xl = S.dma("sync", xtile[b][:], xTt[i], deps=(xt_cons[b] or []), stream="xt%d" % b)
        tvx = S.op("vector", lambda e, b=b: e.tensor_tensor(out=xtile[b][:], in0=xtile[b][:],
                                                            in1=pB[:, 0:1024], op=ALU.subtract),
                   deps=[xl, mub_done], inc=True)
        xc1_done = S.op("vector", lambda e, b=b, i=i: e.tensor_tensor(out=xc1s(i), in0=xtile[b][:],
                                                                      in1=pB[:, 1024:2048], op=ALU.mult),
                        deps=[tvx, rb_done], inc=True)
        xt_cons[b] = [xc1_done]
    cp_all = ("cp", S.counts["cp"])
    td5 = td6 = cp_all
    S.barrier()
    if STOP_AFTER == "p0":
        finish()
        return nc

    wl = S.dma("gpsimd", wbuf[0][:, 0:ND * H].rearrange("p (a h) -> p a h", a=ND),
               wint.rearrange("(a p) h -> p a h", p=128), stream="wb0")
    mm = None
    for i in range(ND):
        for th in range(2):
            mm = S.op("tensor", lambda e, i=i, th=th: e.matmul(
                pB[0:H, 1024 + th * 512:1024 + (th + 1) * 512], wbuf[0][:, i * H:(i + 1) * H],
                xc1s(i, th), start=(i == 0), stop=(i == ND - 1)),
                deps=([wl] if i == 0 else []), inc=(i == ND - 1 and th == 1))
    dt_done = mm
    u2 = S.op("scalar", lambda e: e.activation(out=dtnat, in_=pB[0:H, 1024:2048], func=AF.Relu,
                                               bias=smallc[0:H, 0:1]), deps=[dt_done, c_dtb], inc=True)
    escr2 = ebuf[0][0:H, 1024:2048]
    u3 = S.op("scalar", lambda e: e.activation(out=escr2, in_=pB[0:H, 1024:2048], func=AF.Abs,
                                               bias=smallc[0:H, 0:1]), deps=[dt_done], inc=True)
    u1 = u3
    u4 = S.op("scalar", lambda e: e.activation(out=escr2, in_=escr2, func=AF.Exp, scale=-1.0), deps=[u3], inc=True)
    u5 = S.op("scalar", lambda e: e.activation(out=escr2, in_=escr2, func=AF.Ln, bias=1.0), deps=[u4], inc=True)
    u6 = S.op("vector", lambda e: e.tensor_tensor(out=dtnat, in0=dtnat, in1=escr2, op=ALU.add),
              deps=[u2, u5], inc=True)
    if STOP_AFTER == "dt1":
        S.barrier(); finish(); return nc
    u7 = S.op("vector", lambda e: e.tensor_scalar(out=gnat, in0=dtnat, scalar1=smallc[0:H, 4:5],
                                                  scalar2=-1.0, op0=ALU.mult, op1=ALU.mult),
              deps=[u6, t_expA], inc=True)
    c_reset = S.dma("sync", resetS, resetm, deps=[], stream="cr")
    u8 = S.op("vector", lambda e: e.tensor_tensor_scan(out=Gnat, data0=resetS, data1=gnat,
                                                       initial=0.0, op0=ALU.mult, op1=ALU.add),
              deps=[u7, c_reset], inc=True)
    u9 = S.op("scalar", lambda e: e.activation(out=egnat, in_=Gnat, func=AF.Exp), deps=[u8], inc=True)
    if STOP_AFTER == "dt2":
        S.barrier(); finish(); return nc
    for n in range(NC):
        tt = S.op("tensor", lambda e, n=n: e.transpose(pA[:, n * 32:(n + 1) * 32],
                                                       dtnat[:, n * 128:(n + 1) * 128], identf[0:32, 0:32]),
                  deps=[u6], inc=(n == NC - 1))
    dtT_tok = tt
    for n in range(NC):
        tt = S.op("tensor", lambda e, n=n: e.transpose(pA[:, 256 + n * 32:256 + (n + 1) * 32],
                                                       Gnat[:, n * 128:(n + 1) * 128], identf[0:32, 0:32]),
                  deps=[u8], inc=(n == NC - 1))
    GT_tok = tt
    e1 = S.op("vector", lambda e: e.tensor_copy(out=fv[:], in_=pA[:, 0:256]), deps=[dtT_tok, GT_tok], inc=True)
    e2 = S.op("vector", lambda e: e.tensor_copy(out=ebuf[1][:, 512:768], in_=pA[:, 256:512]), deps=[GT_tok], inc=True)
    e3 = e1
    if STOP_AFTER == "dt3":
        S.barrier(); finish(); return nc
    gf = S.dma("gpsimd", gflat_d.rearrange("o (n h i) -> (o h) n i", n=NC, h=H),
               Gnat.rearrange("h (n i) -> h n i", n=NC), deps=[u8], stream="gm")
    egf = S.dma("gpsimd", egflat_d.rearrange("o (n h i) -> (o h) n i", n=NC, h=H),
                egnat.rearrange("h (n i) -> h n i", n=NC), deps=[u9], stream="gm")
    if STOP_AFTER == "dt3a":
        S.barrier(); finish(); return nc
    glrow = ebuf[1][0:1, 0:256]
    cdrow = ebuf[1][0:1, 256:512]
    gle = S.op("vector", lambda e: e.tensor_copy(out=ebuf[1][0:H, 768:776], in_=Gnat[:, 127::128]),
               deps=[u8], inc=True)
    cde = S.op("vector", lambda e: e.tensor_copy(out=ebuf[1][0:H, 776:784], in_=egnat[:, 127::128]),
               deps=[u9], inc=True)
    glf0 = S.dma("gpsimd", gl_dram, ebuf[1][0:H, 768:776], deps=[gle, u1], stream="gm")
    cdf0 = S.dma("gpsimd", cd_dram, ebuf[1][0:H, 776:784], deps=[cde], stream="gm")
    glf = S.dma("gpsimd", glrow.rearrange("o (n h) -> o n h", n=NC),
                gl_dram.rearrange("h n -> n h").unsqueeze(0), deps=[glf0], slow=True, stream="gm")
    cdf = S.dma("gpsimd", cdrow.rearrange("o (n h) -> o n h", n=NC),
                cd_dram.rearrange("h n -> n h").unsqueeze(0), deps=[cdf0], slow=True, stream="gm")
    if STOP_AFTER == "dt3b":
        S.barrier(); finish(); return nc
    gl1 = S.op("tensor", lambda e: e.matmul(pA[:, 512:768], onesr, glrow, start=True, stop=True),
               deps=[glf, e2], inc=True)
    cd1 = S.op("tensor", lambda e: e.matmul(pA[:, 768:1024], onesr, cdrow, start=True, stop=True),
               deps=[cdf], inc=True)
    if STOP_AFTER == "dt3c":
        S.barrier(); finish(); return nc
    e4 = S.op("vector", lambda e: e.tensor_tensor(out=ET[:], in0=pA[:, 512:768], in1=ebuf[1][:, 512:768],
                                                  op=ALU.subtract), deps=[gl1, cd1, e2], inc=True)
    e5 = S.op("scalar", lambda e: e.activation(out=ET[:], in_=ET[:], func=AF.Exp), deps=[e4], inc=True)
    if STOP_AFTER == "dt3d":
        S.barrier(); finish(); return nc
    e5b = S.op("vector", lambda e: e.tensor_tensor(out=ET[:], in0=ET[:], in1=fv[:], op=ALU.mult),
               deps=[e5, e3], inc=True)
    e6 = S.op("vector", lambda e: e.tensor_copy(out=cdb[:], in_=pA[:, 768:1024]), deps=[cd1], inc=True)
    if STOP_AFTER == "dt4":
        S.barrier(); finish(); return nc
    gh2 = S.dma("gpsimd", glhs[1:33, :], Gnat, deps=[u8], stream="gm")
    gm_all = ("gm", S.counts["gm"])
    gf = egf = glf = cdf = gh2 = gm_all
    S.barrier()
    if STOP_AFTER == "p1a":
        finish()
        return nc

    slab_cons = [None, None]

    def project_T(w, n_otiles, put, use_pB=False, wb=None):
        ps = pB if use_pB else pA
        res = []
        for ot in range(n_otiles):
            b = ot % 2
            buf = (wb or wbuf)[b]
            wld = S.dma("gpsimd", slab3(buf[:, 0:4096]), colslab(w, ot), deps=[slab_cons[b]],
                        stream=("wb%d" % b) if (wb is None) else ("wc%d" % b))
            mm = None
            for i in range(ND):
                for th in range(2):
                    mm = S.op("tensor", lambda e, buf=buf, i=i, th=th, ps=ps: e.matmul(
                        ps[:, th * 512:(th + 1) * 512], buf[:, i * 128:(i + 1) * 128], xc1s(i, th),
                        start=(i == 0), stop=(i == ND - 1)),
                        deps=([wld] if i == 0 else []), inc=(i == ND - 1 and th == 1))
            slab_cons[b] = mm
            tok = put(ot, mm)
            res.append(tok)
            S.wait("tensor", tok)
        return res

    def rope_T(dst, dep0, scratch_bf):
        lo, hi = dst[0:64, :], dst[64:128, :]
        w0 = scratch_bf
        rot, t1, t2 = w0[:, 0:1024], w0[:, 1024:2048], w0[:, 2048:3072]
        a1 = S.op("vector", lambda e: e.tensor_scalar(out=rot[0:64, :], in0=hi, scalar1=-1.0,
                                                      scalar2=None, op0=ALU.mult),
                  deps=[dep0, td5, td6], inc=True)
        a2 = S.op("vector", lambda e: e.tensor_scalar(out=rot[64:128, :], in0=lo, scalar1=1.0,
                                                      scalar2=None, op0=ALU.mult), inc=True)
        a3 = S.op("vector", lambda e: e.tensor_tensor(out=t1, in0=dst, in1=cosD, op=ALU.mult),
                  deps=[a1], inc=True)
        a4 = S.op("vector", lambda e: e.tensor_tensor(out=t2, in0=rot, in1=sinD, op=ALU.mult),
                  deps=[a2], inc=True)
        a5 = S.op("vector", lambda e: e.tensor_tensor(out=dst, in0=t1, in1=t2, op=ALU.add),
                  deps=[a3, a4], inc=True)
        return a5, a5

    def k_put(ot, mm):
        tgt = kT[:, ot * T:(ot + 1) * T]
        ev = S.op("scalar", lambda e, tgt=tgt: e.activation(out=tgt, in_=pA[:, 0:1024], func=AF.Copy),
                  deps=[mm], inc=True)
        a5, a6 = rope_T(tgt, ev, ebuf[0][:].bitcast(BF16))
        S.wait("tensor", a5)
        S.wait("tensor", a6)
        return ev

    def v_put(ot, mm):
        tgt = vnat
        evb = ebuf[0][:].bitcast(BF16)[:, 0:1024]
        ev = S.op("scalar", lambda e, evb=evb: e.activation(out=evb, in_=pA[:, 0:1024], func=AF.Copy),
                  deps=[mm], inc=True)
        for n in range(NC):
            ttk = S.op("tensor", lambda e, n=n, evb=evb: e.transpose(
                pBb[:, 2048 + (n % 8) * 128:2048 + ((n % 8) + 1) * 128],
                evb[:, n * 128:(n + 1) * 128], identb[:]),
                deps=([ev, u1, dt_done] if n == 0 else []), inc=(n == NC - 1))
        evs = S.op("vector", lambda e, ot=ot: e.tensor_copy(
            out=vnat.rearrange("p (n kv d) -> p n kv d", n=NC, kv=KV)[:, :, ot, :],
            in_=pBb[:, 2048:3072].rearrange("p (n d) -> p n d", n=NC)), deps=[ttk], inc=True)
        S.wait("tensor", evs)
        S.wait("scalar", evs)
        return ev

    k_res = project_T(wkt, KV, k_put)
    v_res = project_T(wvt, KV, v_put)
    S.barrier()
    if STOP_AFTER == "p1b":
        finish()
        return nc

    def q_put(ot, mm):
        dst = ebuf[0][:].bitcast(BF16)[:, 0:1024]
        ev = S.op("scalar", lambda e, dst=dst: e.activation(out=dst, in_=pB[:, 0:1024], func=AF.Copy),
                  deps=[mm], inc=True)
        a5, a6 = rope_T(dst, ev, ebuf[1][:].bitcast(BF16))
        sp = S.dma("scalar", q_spill.rearrange("(a p) t -> a p t", p=128)[ot], dst, deps=[a5, a6],
                   stream="qo")
        S.wait("vector", sp)
        S.wait("scalar", sp)
        return ev

    q_res = project_T(wqt, ND, q_put, use_pB=True, wb=wbuf2)
    q_sp_last = ("qo", S.counts["qo"])

    def g_put(ot, mm):
        slb = ebuf[1][:].bitcast(BF16)[:, 0:1024]
        sl = S.op("scalar", lambda e, slb=slb: e.activation(out=slb, in_=pB[:, 0:1024], func=AF.Silu),
                  deps=[mm], inc=True)
        gout = ebuf[0][:].bitcast(BF16)[:, 0:1024]
        ev = S.op("vector", lambda e, slb=slb, gout=gout: e.tensor_scalar(
            out=gout, in0=slb, scalar1=smallc[:, 2:3], scalar2=None, op0=ALU.mult),
            deps=[sl, c_gnw], inc=True)
        sp = S.dma("scalar", sg_spill.rearrange("(a p) t -> a p t", p=128)[ot],
                   gout, deps=[ev], stream="go")
        S.wait("vector", sp)
        S.wait("scalar", sp)
        return sl

    g_res = project_T(wgt, ND, g_put, use_pB=True, wb=wbuf2)
    g_sp_last = ("go", S.counts["go"])
    S.barrier()
    if STOP_AFTER == "p1c":
        finish()
        return nc

    zst = S.op("gpsimd", lambda e: e.memset(state[:], 0.0), inc=True)
    cs_sp = []
    scan_tok = zst
    ce_prev = None
    for n in range(NC):
        for kv in range(KV):
            ttk = S.op("tensor", lambda e, kv=kv, n=n: e.transpose(
                pAb[:, 1024 + kv * 128:1024 + (kv + 1) * 128],
                kT[:, kv * T + n * 128:kv * T + (n + 1) * 128], identb[:]),
                deps=([scan_tok, e5b, ce_prev, zst] if kv == 0 else []), inc=(kv == KV - 1))
        kd = S.op("vector", lambda e, n=n: e.tensor_tensor(
            out=kdec.rearrange("p (kv r d) -> p kv r d", kv=KV, r=4),
            in0=pAb[:, 1024:2048].rearrange("p (kv d) -> p kv d", kv=KV)
                .unsqueeze(2).broadcast_to([128, KV, 4, HD]),
            in1=ET[:, n * H:(n + 1) * H].rearrange("p (kv r) -> p kv r", kv=KV)
                .unsqueeze(3).broadcast_to([128, KV, 4, HD]),
            op=ALU.mult), deps=[ttk], inc=True)
        S.wait("tensor", kd)
        cs_toks = []
        for h in range(H):
            ps = pA if h < 16 else pB
            off = (h % 16) * 128
            tm = S.op("tensor", lambda e, h=h, ps=ps, off=off, n=n: e.matmul(
                ps[:, off:off + 128], kdec[:, h * 128:(h + 1) * 128],
                vnat[:, n * 1024 + (h // 4) * 128:n * 1024 + (h // 4 + 1) * 128],
                start=True, stop=True), deps=([kd] if h == 0 else []), inc=(h in (15, 31)))
            if h in (15, 31):
                cs_toks.append(tm)
        ce1 = S.op("scalar", lambda e: e.activation(out=csbuf[:, 0:2048], in_=pA[:], func=AF.Copy),
                   deps=[cs_toks[0]] + ([cs_sp[-1]] if cs_sp else []), inc=True)
        ce2 = S.op("scalar", lambda e: e.activation(out=csbuf[:, 2048:4096], in_=pB[:], func=AF.Copy),
                   deps=[cs_toks[1]], inc=True)
        ce_prev = ce2
        csp = S.dma("scalar", cs_spill[n], csbuf[:], deps=[ce1, ce2], stream="co")
        cs_sp.append(csp)
        s1 = S.op("vector", lambda e, n=n: e.tensor_tensor(
            out=state[:].rearrange("p (h d) -> p h d", h=H),
            in0=state[:].rearrange("p (h d) -> p h d", h=H),
            in1=cdb[:, n * H:(n + 1) * H].unsqueeze(2).broadcast_to([128, H, HD]),
            op=ALU.mult), deps=[e6, kd], inc=True)
        s2 = S.op("vector", lambda e: e.tensor_tensor(out=state[:, 0:2048], in0=state[:, 0:2048],
                                                      in1=pA[:], op=ALU.add), deps=[s1, cs_toks[0]], inc=True)
        s3 = S.op("vector", lambda e: e.tensor_tensor(out=state[:, 2048:4096], in0=state[:, 2048:4096],
                                                      in1=pB[:], op=ALU.add), deps=[s1, cs_toks[1]], inc=True)
        scan_tok = s3
        S.wait("tensor", s2)
        S.wait("tensor", s3)
    st_out = S.dma("gpsimd", cc_in.ap().rearrange("(h d) e -> d h e", h=H),
                   state[:].rearrange("d (h e) -> d h e", h=H), deps=[scan_tok], stream="gm")
    S.waits("gpsimd", [st_out])
    S.counts["sC"] += 1
    cc_tok = ("sC", S.counts["sC"])

    def cc_fn(e):
        return e.collective_compute("AllGather", ALU.bypass, replica_groups=replica_groups,
                                    ins=[cc_in.ap().opt()], outs=[cc_out.ap().opt()])
    S.items["gpsimd"].append(["op", cc_fn, "sC"])
    tld = S.dma("gpsimd", state[:].rearrange("d (h e) -> d h e", h=H),
                cc_out.ap()[0:H * HD, :].rearrange("(h d) e -> d h e", h=H), deps=[cc_tok], stream="gm")
    tc0 = S.op("vector", lambda e: e.tensor_scalar(out=state[:], in0=state[:], scalar1=smallc[:, 3:4],
                                                   scalar2=None, op0=ALU.mult), deps=[tld, c_ccol], inc=True)
    tb0 = S.op("scalar", lambda e: e.activation(out=state_bf[:], in_=state[:], func=AF.Copy),
               deps=[tc0], inc=True)
    S.barrier()
    if STOP_AFTER == "p2":
        finish()
        return nc

    c_negb = S.dma("sync", grhs[1:33, :], negblock, stream="cn")
    qsp = q_spill.rearrange("(a p) t -> p a t", p=128)

    def load_q(n, deps):
        tok = None
        for gidx in range(8):
            tok = S.dma("sync", qbuf[:, gidx * 512:(gidx + 1) * 512].rearrange("p (a t) -> p a t", a=4),
                        qsp[:, gidx * 4:(gidx + 1) * 4, n * 128:(n + 1) * 128],
                        deps=deps if gidx == 0 else [], stream="qb")
        return tok

    qld = load_q(0, [q_sp_last])
    csld = S.dma("sync", csbuf[:], cs_spill[0], deps=[cs_sp[-1]], stream="cl")
    grl = S.dma("gpsimd", grhs[0:1, :], gflat_d[:, 0:4096], deps=[gf, c_negb], stream="gr")
    st_up_prev = tc0
    sbf_prev = tb0
    pa_free = None
    pb_free = None
    egl_prev = None
    for n in range(NC):
        qld_n, csld_n, grl_n = qld, csld, grl
        ymm_b = None
        dmm = None
        for hf in range(2):
            egl = S.dma("gpsimd", egrow, egflat_d[:, n * 4096 + hf * 2048:n * 4096 + (hf + 1) * 2048],
                        deps=[egf, egl_prev], stream="eg")
            dmm = None
            for sl in range(4):
                dmm = S.op("tensor", lambda e, sl=sl, hf=hf, n=n: e.matmul(
                    pA[:, sl * 512:(sl + 1) * 512], glhs[:, n * 128:(n + 1) * 128],
                    grhs[:, hf * 2048 + sl * 512:hf * 2048 + (sl + 1) * 512],
                    start=True, stop=False),
                    deps=([grl_n, gh1, gh2, pa_free] if sl == 0 else []))
            for sl in range(4):
                dmm = S.op("tensor", lambda e, sl=sl: e.matmul(
                    pA[:, sl * 512:(sl + 1) * 512], slT[:],
                    negidS[:].unsqueeze(1).broadcast_to([128, 4, 128]),
                    start=False, stop=True), deps=([c_sl, c_negid] if sl == 0 else []), inc=(sl == 3))
            dc = S.op("scalar", lambda e, hf=hf: e.activation(out=decayT[:, hf * 2048:(hf + 1) * 2048],
                                                              in_=pA[:], func=AF.Exp), deps=[dmm], inc=True)
            if DEBUG_SPILL and n == 1:
                S.dma("scalar", dbg_dc[:, hf * 2048:(hf + 1) * 2048], decayT[:, hf * 2048:(hf + 1) * 2048],
                      deps=[dc], stream="ro")
            smm = None
            for h in range(16):
                hh = hf * 16 + h
                smm = S.op("tensor", lambda e, hh=hh, h=h, n=n: e.matmul(
                    pA[:, h * 128:(h + 1) * 128], kT[:, (hh // 4) * T + n * 128:(hh // 4) * T + (n + 1) * 128],
                    qbuf[:, hh * 128:(hh + 1) * 128], start=True, stop=True),
                    deps=([dc, qld_n] if h == 0 else []), inc=(h == 15))
            ap_ = S.op("vector", lambda e, hf=hf: e.tensor_tensor(
                out=apT[:, hf * 2048:(hf + 1) * 2048], in0=pA[:], in1=decayT[:, hf * 2048:(hf + 1) * 2048],
                op=ALU.mult), deps=[smm, dc], inc=True)
            ap2 = S.op("vector", lambda e, hf=hf, n=n: e.tensor_tensor(
                out=apT[:, hf * 2048:(hf + 1) * 2048].rearrange("p (h i) -> p h i", h=16),
                in0=apT[:, hf * 2048:(hf + 1) * 2048].rearrange("p (h i) -> p h i", h=16),
                in1=fv[:, n * H + hf * 16:n * H + (hf + 1) * 16].unsqueeze(2).broadcast_to([128, 16, 128]),
                op=ALU.mult), deps=[ap_, e3], inc=True)
            if DEBUG_SPILL and n == 1:
                S.dma("scalar", dbg_ap[:, hf * 2048:(hf + 1) * 2048], apT[:, hf * 2048:(hf + 1) * 2048],
                      deps=[ap2], stream="ro")
            ymm_a = None
            for h in range(16):
                hh = hf * 16 + h
                ymm_a = S.op("tensor", lambda e, hh=hh, h=h, n=n: e.matmul(
                    pA[:, h * 128:(h + 1) * 128],
                    vnat[:, n * 1024 + (hh // 4) * 128:n * 1024 + (hh // 4 + 1) * 128],
                    apT[:, hh * 128:(hh + 1) * 128], start=True, stop=True),
                    deps=([ap2] if h == 0 else []), inc=(h == 15))
            for h in range(16):
                hh = hf * 16 + h
                ymm_b = S.op("tensor", lambda e, hh=hh, h=h: e.matmul(
                    pB[:, h * 128:(h + 1) * 128], state_bf[:, hh * 128:(hh + 1) * 128],
                    qbuf[:, hh * 128:(hh + 1) * 128], start=True, stop=True),
                    deps=([sbf_prev, pb_free] if h == 0 else []), inc=(h == 15))
            yi = S.op("scalar", lambda e: e.activation(out=ebuf[0][:].bitcast(BF16)[:, 0:2048], in_=pB[:],
                                                       func=AF.Copy), deps=[ymm_b], inc=True)
            if DEBUG_SPILL and n == 1:
                S.dma("scalar", dbg_yi[:, hf * 2048:(hf + 1) * 2048], ebuf[0][:].bitcast(BF16)[:, 0:2048],
                      deps=[yi], stream="ro")
            ebmm = None
            for sl in range(4):
                ebmm = S.op("tensor", lambda e, sl=sl: e.matmul(
                    pB[:, sl * 512:(sl + 1) * 512], onesb[0:1, :],
                    egrow[:, sl * 512:(sl + 1) * 512], start=True, stop=True),
                    deps=([yi, egl] if sl == 0 else []), inc=(sl == 3))
            egl_prev = ebmm
            y1 = S.op("vector", lambda e: e.tensor_tensor(out=ebuf[1][:, 0:2048],
                                                          in0=ebuf[0][:].bitcast(BF16)[:, 0:2048],
                                                          in1=pB[:], op=ALU.mult), deps=[ebmm, yi], inc=True)
            y2 = S.op("vector", lambda e: e.tensor_tensor(out=ebuf[1][:, 0:2048], in0=ebuf[1][:, 0:2048],
                                                          in1=pA[:], op=ALU.add), deps=[y1, ymm_a], inc=True)
            pa_free = y2
            sq = S.op("vector", lambda e: e.tensor_tensor(out=ebuf[0][:].bitcast(BF16)[:, 2048:4096],
                                                          in0=ebuf[1][:, 0:2048], in1=ebuf[1][:, 0:2048],
                                                          op=ALU.mult), deps=[y2], inc=True)
            rmm = None
            for sl in range(4):
                rmm = S.op("tensor", lambda e, sl=sl: e.matmul(
                    pB[:, sl * 512:(sl + 1) * 512], onesb[:],
                    ebuf[0][:].bitcast(BF16)[:, 2048 + sl * 512:2048 + (sl + 1) * 512],
                    start=True, stop=True), deps=([sq, y1] if sl == 0 else []), inc=(sl == 3))
            r1 = S.op("scalar", lambda e: e.activation(out=ebuf[0][:, 0:2048], in_=pB[:], func=AF.Ln,
                                                       scale=1.0 / HD, bias=RMS_EPS), deps=[rmm, yi, c_cv], inc=True)
            r2 = S.op("scalar", lambda e: e.activation(out=ebuf[0][:, 0:2048], in_=ebuf[0][:, 0:2048],
                                                       func=AF.Exp, scale=-0.5), deps=[r1], inc=True)
            pb_free = r1
            yt = S.op("vector", lambda e, n=n, hf=hf: e.tensor_tensor(
                out=ynT.rearrange("p (h nn i) -> p h nn i", h=H, nn=NC)[:, hf * 16:(hf + 1) * 16, n, :],
                in0=ebuf[1][:, 0:2048].rearrange("p (h i) -> p h i", h=16),
                in1=ebuf[0][:, 0:2048].rearrange("p (h i) -> p h i", h=16), op=ALU.mult),
                deps=[y2, r2], inc=True)
        if n + 1 < NC:
            qld = load_q(n + 1, [ymm_b])
            grl = S.dma("gpsimd", grhs[0:1, :], gflat_d[:, (n + 1) * 4096:(n + 2) * 4096], deps=[dmm],
                        stream="gr")
        su1 = S.op("vector", lambda e, n=n: e.tensor_tensor(
            out=state[:].rearrange("p (h d) -> p h d", h=H),
            in0=state[:].rearrange("p (h d) -> p h d", h=H),
            in1=cdb[:, n * H:(n + 1) * H].unsqueeze(2).broadcast_to([128, H, HD]),
            op=ALU.mult), deps=[e6, sbf_prev, st_up_prev], inc=True)
        su2 = S.op("vector", lambda e: e.tensor_tensor(out=state[:], in0=state[:], in1=csbuf[:],
                                                       op=ALU.add), deps=[su1, csld_n], inc=True)
        st_up_prev = su2
        if n + 1 < NC:
            csld = S.dma("sync", csbuf[:], cs_spill[n + 1], deps=[su2], stream="cl")
            sbf = S.op("scalar", lambda e: e.activation(out=state_bf[:], in_=state[:], func=AF.Copy),
                       deps=[su2, ymm_b], inc=True)
            sbf_prev = sbf
    if DEBUG_SPILL:
        S.dma("scalar", dbg_yn, ynT, deps=[ydone_prev] if False else [yt], stream="ro")
    S.barrier()
    if STOP_AFTER == "p3":
        finish()
        return nc

    sgsp = sg_spill.rearrange("(a p) t -> p a t", p=128)
    sg_cons = [None, None]
    ygt = None
    for idx in range(16):
        th, hg = idx % 2, idx // 2
        slot = idx % 2
        sgl = S.dma("sync", sgbuf[:, slot * 2048:(slot + 1) * 2048].rearrange("p (a t) -> p a t", a=4),
                    sgsp[:, hg * 4:(hg + 1) * 4, th * 512:(th + 1) * 512],
                    deps=[g_sp_last, sg_cons[slot]], stream="sg%d" % slot)
        ygt = S.op("vector", lambda e, th=th, hg=hg, slot=slot: e.tensor_tensor(
            out=ynT.rearrange("p (h t) -> p h t", h=H)[:, hg * 4:(hg + 1) * 4, th * 512:(th + 1) * 512],
            in0=ynT.rearrange("p (h t) -> p h t", h=H)[:, hg * 4:(hg + 1) * 4, th * 512:(th + 1) * 512],
            in1=sgbuf[:, slot * 2048:(slot + 1) * 2048].rearrange("p (a t) -> p a t", a=4),
            op=ALU.mult), deps=[sgl], inc=True)
        sg_cons[slot] = ygt
    gate_all = ygt

    c_fs = S.dma("sync", smallc[:, 5:9] if False else f_all[:, 0:128], wq_fs, stream="c5")
    c_zs = S.dma("sync", f_all[:, 128:256], wq_zs, stream="c5")
    c_s2s = S.dma("sync", f_all[:, 256:260], wq_s2s, stream="c5")
    c5b = ("c5", S.counts["c5"])
    ff1 = S.op("vector", lambda e: e.tensor_tensor(
        out=f_all[:, 0:128].rearrange("p (j g) -> p j g", j=4),
        in0=f_all[:, 0:128].rearrange("p (j g) -> p j g", j=4),
        in1=f_all[:, 256:260].unsqueeze(2).broadcast_to([128, 4, 32]),
        op=ALU.mult), deps=[c5b], inc=True)
    qslab = big2.ap()[:, 0:4096]
    mslab = big2.ap()[:, 4096:8192]
    dqs = big3.ap().bitcast(F32)
    wts = big6.ap()
    _ffn_state = {"fo": None, "dq": None}

    def emit_ffn_split_j(j):
        fql = S.dma("gpsimd", qslab, wq_qs[j], deps=[_ffn_state["dq"]], stream="fq")
        fml = S.dma("gpsimd", mslab, wq_ms[j], deps=[_ffn_state["dq"]], stream="fm")
        fd1 = S.op("vector", lambda e, j=j: e.tensor_tensor(
            out=dqs.rearrange("p (g i) -> p g i", g=32),
            in0=qslab.rearrange("p (g i) -> p g i", g=32),
            in1=f_all[:, 128 + j * 32:128 + (j + 1) * 32].unsqueeze(2).broadcast_to([128, 32, 128]),
            op=ALU.subtract), deps=[fql, ff1], inc=True)
        fd2 = S.op("vector", lambda e, j=j: e.tensor_tensor(
            out=dqs.rearrange("p (g i) -> p g i", g=32),
            in0=dqs.rearrange("p (g i) -> p g i", g=32),
            in1=f_all[:, j * 32:(j + 1) * 32].unsqueeze(2).broadcast_to([128, 32, 128]),
            op=ALU.mult), deps=[fd1], inc=True)
        fd3 = S.op("vector", lambda e: e.tensor_tensor(out=qslab, in0=dqs, in1=mslab,
                                                       op=ALU.mult), deps=[fd2, fml], inc=True)
        tev3 = []
        for nb in range(32):
            fttk = S.op("tensor", lambda e, nb=nb: e.transpose(
                pAb[:, 2048 + (nb % 4) * 128:2048 + ((nb % 4) + 1) * 128],
                qslab[:, nb * 128:(nb + 1) * 128], identb[:]),
                deps=([fd3, _ffn_state["fo"]] if nb == 0 else []), inc=(nb % 4 == 3))
            if nb % 4 == 3:
                fev = S.op("vector", lambda e, nb=nb: e.tensor_copy(
                    out=wts[:, (nb - 3) * 128:(nb + 1) * 128], in_=pAb[:, 2048:2560]),
                    deps=[fttk], inc=True)
                tev3.append(fev)
                S.wait("tensor", fev)
        _ffn_state["dq"] = fd3
        _ffn_state["fo"] = S.dma("gpsimd", wdq_in.rearrange("(j p) n -> j p n", p=128)[j], wts[:],
                                 deps=tev3, stream="fo")


    wo_cons = [None, None]
    xt_cons = [None, None]
    xsp_cons = [None, None]
    st2 = None
    stat_cons = None
    x2_last = None
    for ot in range(ND):
        bb = ot % 2
        if ot >= 4 and ot % 4 == 0 and (ot - 4) // 4 < 4:
            emit_ffn_split_j((ot - 4) // 4)
        wld = S.dma("gpsimd", slab3(wbuf[bb][:, 0:4096]), colslab(wot, ot), deps=[wo_cons[bb], tb0],
                    stream="wb%d" % bb)
        mm = None
        for th in range(2):
            for hd in range(ND):
                mm = S.op("tensor", lambda e, bb=bb, th=th, hd=hd: e.matmul(
                    pA[:, th * 512:(th + 1) * 512], wbuf[bb][:, hd * 128:(hd + 1) * 128],
                    ynTv(hd, th * 512, (th + 1) * 512), start=(hd == 0), stop=(hd == ND - 1)),
                    deps=([wld, gate_all] if (th == 0 and hd == 0) else []),
                    inc=(th == 1 and hd == ND - 1))
        wo_cons[bb] = mm
        xl = S.dma("sync", xtile[bb][:], xTt[ot], deps=[xt_cons[bb], xsp_cons[bb]], stream="xt%d" % bb)
        x2v = S.op("vector", lambda e, bb=bb: e.tensor_tensor(out=xtile[bb][:], in0=xtile[bb][:],
                                                              in1=pA[:, 0:1024], op=ALU.add),
                   deps=[xl, mm], inc=True)
        S.wait("tensor", x2v)
        xsp = S.dma("scalar", x2t[ot], xtile[bb][:], deps=[x2v], stream="x2o")
        xsp_cons[bb] = xsp
        x2_last = xsp
        cb = S.op("vector", lambda e, bb=bb: e.tensor_copy(out=ebuf[0][:].bitcast(BF16)[:, 0:1024],
                                                           in_=xtile[bb][:]), deps=[x2v, stat_cons], inc=True)
        cs = S.op("scalar", lambda e, bb=bb: e.activation(out=ebuf[0][:].bitcast(BF16)[:, 1024:2048],
                                                          in_=xtile[bb][:], func=AF.Square),
                  deps=[x2v, stat_cons, xsp], inc=True)
        xt_cons[bb] = cs
        for th in range(2):
            S.op("tensor", lambda e, th=th, ot=ot: e.matmul(
                pB[0:1, th * 512:(th + 1) * 512], onesb[:, 0:1],
                ebuf[0][:].bitcast(BF16)[:, th * 512:(th + 1) * 512],
                start=(ot == 0), stop=(ot == ND - 1)), deps=[cb])
            st2 = S.op("tensor", lambda e, th=th, ot=ot: e.matmul(
                pB[0:1, 1024 + th * 512:1024 + (th + 1) * 512], onesb[:, 0:1],
                ebuf[0][:].bitcast(BF16)[:, 1024 + th * 512:1024 + (th + 1) * 512],
                start=(ot == 0), stop=(ot == ND - 1)), deps=[cs], inc=True)
        stat_cons = st2
    S.waits("gpsimd", [_ffn_state["fo"]])
    S.counts["sC"] += 1
    cc2_tok = ("sC", S.counts["sC"])

    def cc2_fn(e):
        return e.collective_compute("AllGather", ALU.bypass, replica_groups=[list(range(8))],
                                    ins=[wdq_in.ap().opt()], outs=[wdq_out.ap().opt()])
    S.items["gpsimd"].append(["op", cc2_fn, "sC"])
    S.barrier()

    mu2 = ebuf[1][0:1, 0:1024]
    r2r = ebuf[1][0:1, 1024:2048]
    w1 = ebuf[0][0:1, 0:1024]
    w2 = ebuf[0][0:1, 1024:2048]
    m1 = S.op("vector", lambda e: e.tensor_scalar(out=mu2, in0=pB[0:1, 0:1024], scalar1=1.0 / D,
                                                  scalar2=None, op0=ALU.mult), deps=[st2], inc=True)
    m2 = S.op("vector", lambda e: e.tensor_scalar(out=w1, in0=pB[0:1, 1024:2048], scalar1=1.0 / D,
                                                  scalar2=None, op0=ALU.mult), deps=[st2], inc=True)
    m3 = S.op("vector", lambda e: e.scalar_tensor_tensor(out=w2, in0=mu2, scalar=-1.0, in1=mu2,
                                                         op0=ALU.mult, op1=ALU.mult), deps=[m1], inc=True)
    m4 = S.op("vector", lambda e: e.tensor_tensor(out=w1, in0=w1, in1=w2, op=ALU.add), deps=[m2, m3], inc=True)
    m5 = S.op("scalar", lambda e: e.activation(out=w2, in_=w1, func=AF.Ln, bias=LN_EPS), deps=[m4, c_cv], inc=True)
    m6 = S.op("scalar", lambda e: e.activation(out=r2r, in_=w2, func=AF.Exp, scale=-0.5), deps=[m5], inc=True)

    mo2 = None
    for th in range(2):
        mo2 = S.op("tensor", lambda e, th=th: e.matmul(pB[:, th * 512:(th + 1) * 512], onesr,
                                                       mu2[:, th * 512:(th + 1) * 512], start=True, stop=True),
                   deps=[m1, m6, st2], inc=(th == 1))
    ro2 = None
    for th in range(2):
        ro2 = S.op("tensor", lambda e, th=th: e.matmul(pB[:, 1024 + th * 512:1024 + (th + 1) * 512],
                                                       onesr, r2r[:, th * 512:(th + 1) * 512],
                                                       start=True, stop=True), deps=[m6, m2], inc=(th == 1))
    rbe = S.op("vector", lambda e: e.tensor_copy(out=rbr[:], in_=pB[:, 1024:2048]), deps=[ro2], inc=True)

    xc2_all = None
    for i in range(ND):
        bb = i % 2
        xl = S.dma("sync", xtile[bb][:], x2t[i], deps=[x2_last, xt_cons[bb]], stream="xt%d" % bb)
        tv = S.op("vector", lambda e, bb=bb: e.tensor_tensor(out=xtile[bb][:], in0=xtile[bb][:],
                                                             in1=pB[:, 0:1024], op=ALU.subtract),
                  deps=[xl, mo2, gate_all], inc=True)
        tv2 = S.op("vector", lambda e, bb=bb, i=i: e.tensor_tensor(out=xc1s(i), in0=xtile[bb][:],
                                                                   in1=rbr[:], op=ALU.mult),
                   deps=[tv, rbe], inc=True)
        xt_cons[bb] = tv2
        xc2_all = tv2
    S.barrier()
    if STOP_AFTER == "p4":
        finish()
        return nc

    biasrow = big8.ap()[0:1, 4096:8192]
    c_bias = S.dma("gpsimd", biasrow, wq_b, stream="cb")
    cb_all = ("cb", S.counts["cb"])
    c_bias = cb_all
    wq_cons = [None, None]
    out_toks = []
    outr2 = wdq_out.rearrange("(kt p) n -> kt p n", p=128)
    o2_cons = [None, None]
    o1_tok = None
    for kt in range(32):
        bb = kt % 2
        wl5 = S.dma("sync", wbuf[bb][:], outr2[kt], deps=[wq_cons[bb], cc2_tok], stream="w5%d" % bb)
        xl = S.dma("sync", xtile[bb][:], x2t[kt], deps=[o2_cons[bb]], stream="xt%d" % bb)
        for th in range(2):
            mm = None
            for nb in range(32):
                mm = S.op("tensor", lambda e, th=th, nb=nb, bb=bb: e.matmul(
                    pB[:, th * 512:(th + 1) * 512], wbuf[bb][:, nb * 128:(nb + 1) * 128],
                    xc1s(nb, th), start=(nb == 0), stop=False),
                    deps=([wl5, o1_tok, xc2_all] if nb == 0 else []))
            mm = S.op("tensor", lambda e, th=th, kt=kt: e.matmul(
                pB[:, th * 512:(th + 1) * 512], biasrow[0:1, kt * 128:(kt + 1) * 128],
                onesb[0:1, :].unsqueeze(1).broadcast_to([1, 4, 128]),
                start=False, stop=True), deps=[c_bias], inc=True)
            o2 = S.op("vector", lambda e, th=th, bb=bb: e.tensor_tensor(
                out=ebuf[1][:, 0:512], in0=pB[:, th * 512:(th + 1) * 512],
                in1=xtile[bb][:, th * 512:(th + 1) * 512], op=ALU.add),
                deps=[mm, xl] + ([out_toks[-1]] if out_toks else []), inc=True)
            osp = S.dma("scalar", outr[kt][:, th * 512:(th + 1) * 512], ebuf[1][:, 0:512], deps=[o2],
                        stream="oo")
            out_toks.append(osp)
            o1_tok = o2
        wq_cons[bb] = mm
        o2_cons[bb] = o2

    S.wait("scalar", ("oo", S.counts["oo"]))
    S.wait("scalar", ("x2o", S.counts["x2o"]))

    finish()
    return nc


def _resetm():
    r = np.ones((H, T), np.float32)
    r[:, ::CH] = 0.0
    return r


def make_consts():
    return dict(
        eye_bf=np.eye(128, dtype=BF),
        eye_f32=np.tile(np.eye(32, dtype=np.float32), (4, 1)),
        sl_tri=np.triu(np.ones((128, 128), np.float32), 1).astype(BF),
        negid=(NEG * np.eye(128, dtype=np.float32)).astype(BF),
        negblock=np.repeat(-np.eye(32, dtype=np.float32), 128, axis=1),
        resetm=_resetm(),
        onesbf=np.ones((128, 128), dtype=BF),
        constv=np.tile(np.array([[LN_EPS, 1.0, -PI]], np.float32), (128, 1)),
    )


def make_in_maps(inputs, n_cores=8):
    x = np.asarray(inputs["L0_attn_hidden_states"])

    def TP(a):
        return np.ascontiguousarray(np.asarray(a).T)

    consts = make_consts()
    shared = {
        "wqt": TP(inputs["L0_attn_Wq"]), "wkt": TP(inputs["L0_attn_Wk"]),
        "wvt": TP(inputs["L0_attn_Wv"]), "wgt": TP(inputs["L0_attn_Wg"]),
        "wot": TP(inputs["L0_attn_Wo"]), "wint": TP(inputs["L0_attn_Win"]),
        "dtb": np.ascontiguousarray(np.asarray(inputs["L0_attn_dt_bias"], np.float32).reshape(H, 1)),
        "alog": np.ascontiguousarray(np.asarray(inputs["L0_attn_A_log_bias"], np.float32).reshape(H, 1)),
        "gnw": np.ascontiguousarray(np.asarray(inputs["L0_attn_gnorm_w"], np.float32).reshape(HD, 1)),
        "wq_b": np.ascontiguousarray(np.asarray(inputs["L0_ffn_bias"], np.float32).reshape(1, D)),
    }
    shared.update(consts)
    in_maps = []
    for c in range(n_cores):
        b, half = c // 2, c % 2
        m = dict(shared)
        m["xT"] = np.ascontiguousarray(x[b, half * T:(half + 1) * T, :].T)
        ks = slice(512 * c, 512 * (c + 1))
        m["wq_qs"] = np.ascontiguousarray(np.asarray(inputs["L0_ffn_W_q"], np.int32)[ks].reshape(4, 128, D))
        m["wq_ms"] = np.ascontiguousarray(np.asarray(inputs["L0_ffn_mask"], np.int32)[ks].reshape(4, 128, D))
        scs = np.asarray(inputs["L0_ffn_scales"], np.float32).reshape(D, 32)[ks].reshape(4, 128, 32)
        zrs = np.asarray(inputs["L0_ffn_zeros"], np.float32).reshape(D, 32)[ks].reshape(4, 128, 32)
        s2s = np.asarray(inputs["L0_ffn_scale2"], np.float32).reshape(D)[ks].reshape(4, 128)
        m["wq_fs"] = np.ascontiguousarray(scs.transpose(1, 0, 2).reshape(128, 128))
        m["wq_zs"] = np.ascontiguousarray(zrs.transpose(1, 0, 2).reshape(128, 128))
        m["wq_s2s"] = np.ascontiguousarray(s2s.T)
        posv = half * T + np.arange(T, dtype=np.float64)
        inv = 1.0 / 10000.0 ** (np.arange(0, HD, 2, dtype=np.float64) / HD)
        ang = posv[:, None] * inv
        m["cos_d"] = np.ascontiguousarray(np.tile(np.cos(ang).T, (2, 1)).astype(BF))
        m["sin_d"] = np.ascontiguousarray(np.tile(np.sin(ang).T, (2, 1)).astype(BF))
        m["ccol"] = np.full((128, 1), float(half), np.float32)
        in_maps.append(m)
    return in_maps


def kernel(**inputs):
    x = np.asarray(inputs["L0_attn_hidden_states"])
    B, L, _ = x.shape
    n_cores = 8
    nc = build_program([[0, 1], [2, 3], [4, 5], [6, 7]])
    in_maps = make_in_maps(inputs, n_cores)
    res = run_bass_kernel_spmd(nc, in_maps, core_ids=list(range(n_cores)))
    outp = np.empty((B, L, D), np.float32)
    for c in range(n_cores):
        b, half = c // 2, c % 2
        outp[b, half * T:(half + 1) * T, :] = res.results[c]["out"].T
    return outp


# version 07
# speedup vs baseline: 1.0379x; 1.0379x over previous
import numpy as np
import ml_dtypes
import concourse.bass as bass
import concourse.mybir as mybir
from concourse.bass_utils import run_bass_kernel_spmd
from contextlib import ExitStack

F32 = mybir.dt.float32
BF16 = mybir.dt.bfloat16
I32 = mybir.dt.int32
AF = mybir.ActivationFunctionType
ALU = mybir.AluOpType
BF = ml_dtypes.bfloat16

D = 4096
T = 1024
NC = 8
CH = 128
H = 32
HD = 128
KV = 8
ND = D // 128
LN_EPS = 1e-5
DEBUG_SPILL = False
STOP_AFTER = "all"
RMS_EPS = 1e-5
NEG = -1e30
PI = float(np.pi)

ENGS = ("sync", "gpsimd", "tensor", "vector", "scalar")
ENG_SEM = {"tensor": "sT", "vector": "sV", "scalar": "sA", "gpsimd": "sP"}
DMA_STREAMS = ("cst", "cp", "cr", "cn", "c5", "xt0", "xt1", "wb0", "wb1", "wc0", "wc1",
               "qb", "cl", "gr", "eg", "gm", "sg0", "sg1", "x2o", "qo", "go", "co", "ro", "oo", "cb", "fq", "fm", "fo", "w50", "w51")
ALL_SEMS = DMA_STREAMS + ("sT", "sV", "sA", "sP", "sC")


class Sched:
    def __init__(self):
        self.items = {k: [] for k in ENGS}
        self.counts = {s: 0 for s in ALL_SEMS}
        self.last_wait = {k: {} for k in ENGS}
        self.sems = {}

    def wait(self, eng, tok):
        if tok is None:
            return
        sem, val = tok
        if val <= 0 or val <= self.last_wait[eng].get(sem, 0):
            return
        self.last_wait[eng][sem] = val
        self.items[eng].append(["wait", (sem, val), None])

    def waits(self, eng, toks):
        for t in toks:
            self.wait(eng, t)

    def dma(self, eng, out_ap, in_ap, deps=(), slow=False, stream="cst"):
        self.waits(eng, deps)
        sem = stream
        self.counts[sem] += 16
        self.items[eng].append(["dma", lambda e, o=out_ap, i=in_ap, s=slow: e.dma_start(
            out=o, in_=i, allow_slow_non_contiguous=s), sem])
        return (sem, self.counts[sem])

    def op(self, eng, fn, deps=(), inc=False):
        self.waits(eng, deps)
        if inc:
            sem = ENG_SEM[eng]
            self.counts[sem] += 1
            self.items[eng].append(["op", fn, sem])
            return (sem, self.counts[sem])
        self.items[eng].append(["op", fn, None])
        return None

    def mark(self, eng):
        sem = ENG_SEM.get(eng)
        for item in reversed(self.items[eng]):
            if item[0] == "op":
                if item[2] is None:
                    self.counts[sem] += 1
                    item[2] = sem
                return (item[2], self.counts[item[2]])
            if item[0] == "dma":
                return (item[2], self.counts[item[2]])
        return None

    def barrier(self):
        toks = [self.mark(e) for e in ("tensor", "vector", "scalar", "gpsimd")]
        toks += [(s, self.counts[s]) for s in DMA_STREAMS if self.counts[s] > 0]
        for eng in ENGS:
            self.waits(eng, toks)

    def replay(self, block, sems):
        self.sems = sems
        for eng in ENGS:
            items = self.items[eng]

            def make(items):
                def body(e):
                    for kind, fn, sem in items:
                        if kind == "wait":
                            s, v = fn
                            e.wait_ge(self.sems[s], v)
                        else:
                            inst = fn(e)
                            if sem is not None:
                                inst.then_inc(self.sems[sem], 16 if kind == "dma" else 1)
                return body

            getattr(block, eng)(make(items))


def build_program(replica_groups):
    nc = bass.Bass("TRN2", target_bir_lowering=False, debug=False)
    S = Sched()
    st = ExitStack()

    def SB(name, shape, dtype):
        return st.enter_context(nc.sbuf_tensor(name, shape, dtype))

    EI = "ExternalInput"
    xT = nc.dram_tensor("xT", [D, T], F32, kind=EI).ap()
    wqt = nc.dram_tensor("wqt", [D, D], F32, kind=EI).ap()
    wkt = nc.dram_tensor("wkt", [D, KV * HD], F32, kind=EI).ap()
    wvt = nc.dram_tensor("wvt", [D, KV * HD], F32, kind=EI).ap()
    wgt = nc.dram_tensor("wgt", [D, D], F32, kind=EI).ap()
    wot = nc.dram_tensor("wot", [D, D], F32, kind=EI).ap()
    wint = nc.dram_tensor("wint", [D, H], F32, kind=EI).ap()
    dtb = nc.dram_tensor("dtb", [H, 1], F32, kind=EI).ap()
    alog = nc.dram_tensor("alog", [H, 1], F32, kind=EI).ap()
    gnw = nc.dram_tensor("gnw", [HD, 1], F32, kind=EI).ap()
    wq_qs = nc.dram_tensor("wq_qs", [4, 128, D], I32, kind=EI).ap()
    wq_ms = nc.dram_tensor("wq_ms", [4, 128, D], I32, kind=EI).ap()
    wq_fs = nc.dram_tensor("wq_fs", [128, 4 * 32], F32, kind=EI).ap()
    wq_zs = nc.dram_tensor("wq_zs", [128, 4 * 32], F32, kind=EI).ap()
    wq_s2s = nc.dram_tensor("wq_s2s", [128, 4], F32, kind=EI).ap()
    wq_b = nc.dram_tensor("wq_b", [1, D], F32, kind=EI).ap()
    cosd_d = nc.dram_tensor("cos_d", [128, T], BF16, kind=EI).ap()
    sind_d = nc.dram_tensor("sin_d", [128, T], BF16, kind=EI).ap()
    ccol_d = nc.dram_tensor("ccol", [128, 1], F32, kind=EI).ap()
    eye_bf = nc.dram_tensor("eye_bf", [128, 128], BF16, kind=EI).ap()
    eye_f32 = nc.dram_tensor("eye_f32", [128, 32], F32, kind=EI).ap()
    sl_tri = nc.dram_tensor("sl_tri", [128, 128], BF16, kind=EI).ap()
    negid_d = nc.dram_tensor("negid", [128, 128], BF16, kind=EI).ap()
    negblock = nc.dram_tensor("negblock", [32, D], F32, kind=EI).ap()
    resetm = nc.dram_tensor("resetm", [H, T], F32, kind=EI).ap()
    onesbf = nc.dram_tensor("onesbf", [128, 128], BF16, kind=EI).ap()
    constv = nc.dram_tensor("constv", [128, 3], F32, kind=EI).ap()

    out = nc.dram_tensor("out", [D, T], F32, kind="ExternalOutput").ap()

    q_spill = nc.dram_tensor("q_spill", [H * HD, T], BF16).ap()
    sg_spill = nc.dram_tensor("sg_spill", [H * HD, T], BF16).ap()
    cs_spill = nc.dram_tensor("cs_spill", [NC, 128, H * HD], BF16).ap()
    x2_hbm = nc.dram_tensor("x2_hbm", [D, T], F32).ap()
    gflat_d = nc.dram_tensor("gflat_d", [1, NC * H * CH], F32).ap()
    egflat_d = nc.dram_tensor("egflat_d", [1, NC * H * CH], BF16).ap()
    dbg_yn = nc.dram_tensor("dbg_yn", [128, H * T], BF16).ap()
    dbg_dc = nc.dram_tensor("dbg_dc", [128, H * HD], BF16).ap()
    dbg_ap = nc.dram_tensor("dbg_ap", [128, H * HD], BF16).ap()
    dbg_yi = nc.dram_tensor("dbg_yi", [128, H * HD], BF16).ap()
    gl_dram = nc.dram_tensor("gl_dram", [H, NC], F32).ap()
    cd_dram = nc.dram_tensor("cd_dram", [H, NC], F32).ap()
    cc_in = nc.dram_tensor("cc_in", [H * HD, HD], F32)
    cc_out = nc.dram_tensor("cc_out", [2 * H * HD, HD], F32)
    wdq_in = nc.dram_tensor("wdq_in", [4 * 128, D], BF16)
    wdq_out = nc.dram_tensor("wdq_out", [D, D], BF16)

    big0 = SB("big0", [128, 32768], BF16)
    big1 = SB("big1", [128, 8192], BF16)
    big2 = SB("big2", [128, 8192], BF16)
    big3 = SB("big3", [128, 8192], BF16)
    big5 = SB("big5", [128, 8192], BF16)
    big6 = SB("big6", [128, 4096], BF16)
    big7 = SB("big7", [128, 4096], BF16)
    big8 = SB("big8", [128, 8192], BF16)
    wbuf = [SB(f"wbuf{i}", [128, 4096], BF16) for i in range(2)]
    ebuf = [SB(f"ebuf{i}", [128, 2048], F32) for i in range(2)]
    xtile = [SB(f"xtile{i}", [128, T], F32) for i in range(2)]
    ET = SB("ET", [128, NC * H], F32)
    fv = SB("fv", [128, NC * H], F32)
    cdb = SB("cdb", [128, NC * H], F32)
    identb = SB("identb", [128, 128], BF16)

    identf = SB("identf", [128, 32], F32)
    slT = SB("slT", [128, 128], BF16)
    negidS = SB("negidS", [128, 128], BF16)
    onesb = SB("onesb", [128, 128], BF16)
    smallc = SB("smallc", [128, 16], F32)
    rT = smallc[:, 8:16]

    xc1 = big0.ap()
    ynT = big0.ap()
    kT = big1.ap()
    rbr = big1.ap().bitcast(F32)[:, 0:1024]
    vnat = big2.ap()
    wbuf2 = [big3.ap()[:, 0:4096], big3.ap()[:, 4096:8192]]
    kdec = big3.ap()[:, 0:4096]
    decayT = big3.ap()[:, 0:4096]
    apT = big3.ap()[:, 4096:8192]
    tabf = big5.ap().bitcast(F32)
    cosT = tabf[:, 0:512]
    sinT = tabf[:, 512:1024]
    cosD = big5.ap()[:, 2048:3072]
    sinD = big5.ap()[:, 3072:4096]
    resetS = tabf[0:H, 2048:3072]
    state = big5.ap().bitcast(F32)
    state_bf = big6.ap()
    egrow = big7.ap()[0:1, 2048:4096]
    glhs = big7.ap().bitcast(F32)[0:33, 0:1024]
    onesr = glhs[0:1, 0:128]
    dtp = big8.ap().bitcast(F32)
    dtnat = dtp[0:H, 0:1024]
    gnat = dtp[0:H, 1024:2048]
    Gnat = dtp[0:H, 2048:3072]
    egnat = dtp[0:H, 3072:4096]
    grhs = big8.ap().bitcast(F32)[0:33, 0:4096]
    sgbuf = big8.ap()[:, 0:4096]
    f_all = big1.ap().bitcast(F32)[:, 1024:1324]
    qbuf = wbuf[0].ap()
    csbuf = wbuf[1].ap()

    pA = st.enter_context(nc.psum_tensor("pA", [128, 2048], F32))
    pB = st.enter_context(nc.psum_tensor("pB", [128, 2048], F32))
    pAb = pA.ap().bitcast(BF16)
    pBb = pB.ap().bitcast(BF16)

    nc.const_aps.register(F32, LN_EPS, smallc[:, 5:6])
    nc.const_aps.register(F32, 1.0, smallc[:, 6:7])
    nc.const_aps.register(F32, -PI, smallc[:, 7:8])
    sems = {n: st.enter_context(nc.semaphore(n)) for n in ALL_SEMS}

    def finish():
        with st, nc.Block() as block:
            S.replay(block, sems)

    xTt = xT.rearrange("(n p) t -> n p t", p=128)
    x2t = x2_hbm.rearrange("(n p) t -> n p t", p=128)
    outr = out.rearrange("(n p) t -> n p t", p=128)

    def colslab(w, ot):
        return w.rearrange("(a p) o -> p a o", p=128)[:, :, ot * 128:(ot + 1) * 128]

    def slab3(buf):
        return buf.rearrange("p (a o) -> p a o", a=32)

    def xc1s(i, th=None):
        base = xc1[:, i * T:(i + 1) * T]
        if th is None:
            return base
        return base[:, th * 512:(th + 1) * 512]

    def ynTv(h, lo, hi):
        return ynT.rearrange("p (h t) -> p h t", h=H)[:, h, lo:hi]

    c_eye = S.dma("sync", identb[:], eye_bf)
    c_eyef = S.dma("sync", identf[:], eye_f32)
    c_sl = S.dma("sync", slT[:], sl_tri)
    c_negid = S.dma("sync", negidS[:], negid_d)
    c_ones = S.dma("sync", onesb[:], onesbf)
    gh1 = S.op("gpsimd", lambda e: e.memset(glhs[0:1, :], 1.0), inc=True)
    c_onesr = gh1
    c_cv = S.dma("sync", smallc[:, 5:8], constv)
    c_dtb = S.dma("sync", smallc[0:H, 0:1], dtb)
    c_alog = S.dma("sync", smallc[0:H, 1:2], alog)
    c_gnw = S.dma("sync", smallc[:, 2:3], gnw)
    c_ccol = S.dma("sync", smallc[:, 3:4], ccol_d)
    cst_all = ("cst", S.counts["cst"])
    for nm in ("c_eye", "c_eyef", "c_sl", "c_negid", "c_ones", "c_dtb", "c_alog", "c_gnw",
               "c_ccol", "c_cv", "c_ivf"):
        pass
    c_eye = c_eyef = c_sl = c_negid = c_ones = c_dtb = c_alog = c_gnw = c_ccol = c_cv = cst_all
    t_expA = S.op("scalar", lambda e: e.activation(out=smallc[0:H, 4:5], in_=smallc[0:H, 1:2], func=AF.Exp),
                  deps=[c_alog], inc=True)

    xt_cons = [None, None]
    mm = None
    for i in range(ND):
        b = i % 2
        xl = S.dma("sync", xtile[b][:], xTt[i], deps=(xt_cons[b] or []), stream="xt%d" % b)
        ebb = ebuf[b][:].bitcast(BF16)
        tv = S.op("vector", lambda e, b=b, ebb=ebb: e.tensor_copy(out=ebb[:, 0:T], in_=xtile[b][:]),
                  deps=[xl], inc=True)
        ts_ = S.op("scalar", lambda e, b=b, ebb=ebb: e.activation(out=ebb[:, T:2 * T], in_=xtile[b][:],
                                                                  func=AF.Square), deps=[xl], inc=True)
        xt_cons[b] = [tv, ts_]
        for th in range(2):
            S.op("tensor", lambda e, ebb=ebb, th=th, i=i: e.matmul(
                pA[0:1, th * 512:(th + 1) * 512], onesb[:, 0:1], ebb[:, th * 512:(th + 1) * 512],
                start=(i == 0), stop=(i == ND - 1)), deps=[tv, c_ones])
            mm = S.op("tensor", lambda e, ebb=ebb, th=th, i=i: e.matmul(
                pA[0:1, 1024 + th * 512:1024 + (th + 1) * 512], onesb[:, 0:1],
                ebb[:, 1024 + th * 512:1024 + (th + 1) * 512],
                start=(i == 0), stop=(i == ND - 1)), deps=[ts_], inc=(i == ND - 1 and th == 1))
    stats_done = mm

    murow = ebuf[1][0:1, 0:1024]
    rrow = ebuf[1][0:1, 1024:2048]
    v1row = ebuf[0][0:1, 0:1024]
    v2row = ebuf[0][0:1, 1024:2048]
    t1 = S.op("vector", lambda e: e.tensor_scalar(out=murow, in0=pA[0:1, 0:1024], scalar1=1.0 / D,
                                                  scalar2=None, op0=ALU.mult), deps=[stats_done], inc=True)
    t2 = S.op("vector", lambda e: e.tensor_scalar(out=v1row, in0=pA[0:1, 1024:2048], scalar1=1.0 / D,
                                                  scalar2=None, op0=ALU.mult), deps=[stats_done], inc=True)
    t3 = S.op("vector", lambda e: e.scalar_tensor_tensor(out=v2row, in0=murow, scalar=-1.0, in1=murow,
                                                         op0=ALU.mult, op1=ALU.mult), deps=[t1], inc=True)
    t4 = S.op("vector", lambda e: e.tensor_tensor(out=v1row, in0=v1row, in1=v2row, op=ALU.add),
              deps=[t2, t3], inc=True)
    t5 = S.op("scalar", lambda e: e.activation(out=v2row, in_=v1row, func=AF.Ln, bias=LN_EPS),
              deps=[t4, c_cv], inc=True)
    t6 = S.op("scalar", lambda e: e.activation(out=rrow, in_=v2row, func=AF.Exp, scale=-0.5),
              deps=[t5], inc=True)
    mo = None
    for th in range(2):
        mo = S.op("tensor", lambda e, th=th: e.matmul(pB[:, th * 512:(th + 1) * 512], onesr,
                                                      murow[:, th * 512:(th + 1) * 512], start=True, stop=True),
                  deps=[t1, c_onesr], inc=(th == 1))
    mub_done = mo
    ro = None
    for th in range(2):
        ro = S.op("tensor", lambda e, th=th: e.matmul(pB[:, 1024 + th * 512:1024 + (th + 1) * 512],
                                                      onesr, rrow[:, th * 512:(th + 1) * 512],
                                                      start=True, stop=True), deps=[t6], inc=(th == 1))
    rb_done = ro

    c_cos = S.dma("sync", cosD, cosd_d, deps=[], stream="cp")
    c_sin = S.dma("sync", sinD, sind_d, deps=[], stream="cp")
    xc1_done = None
    for i in range(ND):
        b = i % 2
        xl = S.dma("sync", xtile[b][:], xTt[i], deps=(xt_cons[b] or []), stream="xt%d" % b)
        tvx = S.op("vector", lambda e, b=b: e.tensor_tensor(out=xtile[b][:], in0=xtile[b][:],
                                                            in1=pB[:, 0:1024], op=ALU.subtract),
                   deps=[xl, mub_done], inc=True)
        xc1_done = S.op("vector", lambda e, b=b, i=i: e.tensor_tensor(out=xc1s(i), in0=xtile[b][:],
                                                                      in1=pB[:, 1024:2048], op=ALU.mult),
                        deps=[tvx, rb_done], inc=True)
        xt_cons[b] = [xc1_done]
    cp_all = ("cp", S.counts["cp"])
    td5 = td6 = cp_all
    S.barrier()
    if STOP_AFTER == "p0":
        finish()
        return nc

    wl = S.dma("gpsimd", wbuf[0][:, 0:ND * H].rearrange("p (a h) -> p a h", a=ND),
               wint.rearrange("(a p) h -> p a h", p=128), stream="wb0")
    mm = None
    for i in range(ND):
        for th in range(2):
            mm = S.op("tensor", lambda e, i=i, th=th: e.matmul(
                pB[0:H, 1024 + th * 512:1024 + (th + 1) * 512], wbuf[0][:, i * H:(i + 1) * H],
                xc1s(i, th), start=(i == 0), stop=(i == ND - 1)),
                deps=([wl] if i == 0 else []), inc=(i == ND - 1 and th == 1))
    dt_done = mm
    u2 = S.op("scalar", lambda e: e.activation(out=dtnat, in_=pB[0:H, 1024:2048], func=AF.Relu,
                                               bias=smallc[0:H, 0:1]), deps=[dt_done, c_dtb], inc=True)
    escr2 = ebuf[0][0:H, 1024:2048]
    u3 = S.op("scalar", lambda e: e.activation(out=escr2, in_=pB[0:H, 1024:2048], func=AF.Abs,
                                               bias=smallc[0:H, 0:1]), deps=[dt_done], inc=True)
    u1 = u3
    u4 = S.op("scalar", lambda e: e.activation(out=escr2, in_=escr2, func=AF.Exp, scale=-1.0), deps=[u3], inc=True)
    u5 = S.op("scalar", lambda e: e.activation(out=escr2, in_=escr2, func=AF.Ln, bias=1.0), deps=[u4], inc=True)
    u6 = S.op("vector", lambda e: e.tensor_tensor(out=dtnat, in0=dtnat, in1=escr2, op=ALU.add),
              deps=[u2, u5], inc=True)
    if STOP_AFTER == "dt1":
        S.barrier(); finish(); return nc
    u7 = S.op("vector", lambda e: e.tensor_scalar(out=gnat, in0=dtnat, scalar1=smallc[0:H, 4:5],
                                                  scalar2=-1.0, op0=ALU.mult, op1=ALU.mult),
              deps=[u6, t_expA], inc=True)
    c_reset = S.dma("sync", resetS, resetm, deps=[], stream="cr")
    u8 = S.op("vector", lambda e: e.tensor_tensor_scan(out=Gnat, data0=resetS, data1=gnat,
                                                       initial=0.0, op0=ALU.mult, op1=ALU.add),
              deps=[u7, c_reset], inc=True)
    u9 = S.op("scalar", lambda e: e.activation(out=egnat, in_=Gnat, func=AF.Exp), deps=[u8], inc=True)
    if STOP_AFTER == "dt2":
        S.barrier(); finish(); return nc
    for n in range(NC):
        tt = S.op("tensor", lambda e, n=n: e.transpose(pA[:, n * 32:(n + 1) * 32],
                                                       dtnat[:, n * 128:(n + 1) * 128], identf[0:32, 0:32]),
                  deps=[u6], inc=(n == NC - 1))
    dtT_tok = tt
    for n in range(NC):
        tt = S.op("tensor", lambda e, n=n: e.transpose(pA[:, 256 + n * 32:256 + (n + 1) * 32],
                                                       Gnat[:, n * 128:(n + 1) * 128], identf[0:32, 0:32]),
                  deps=[u8], inc=(n == NC - 1))
    GT_tok = tt
    e1 = S.op("vector", lambda e: e.tensor_copy(out=fv[:], in_=pA[:, 0:256]), deps=[dtT_tok, GT_tok], inc=True)
    e2 = S.op("vector", lambda e: e.tensor_copy(out=ebuf[1][:, 512:768], in_=pA[:, 256:512]), deps=[GT_tok], inc=True)
    e3 = e1
    if STOP_AFTER == "dt3":
        S.barrier(); finish(); return nc
    gf = S.dma("gpsimd", gflat_d.rearrange("o (n h i) -> (o h) n i", n=NC, h=H),
               Gnat.rearrange("h (n i) -> h n i", n=NC), deps=[u8], stream="gm")
    egf = S.dma("gpsimd", egflat_d.rearrange("o (n h i) -> (o h) n i", n=NC, h=H),
                egnat.rearrange("h (n i) -> h n i", n=NC), deps=[u9], stream="gm")
    if STOP_AFTER == "dt3a":
        S.barrier(); finish(); return nc
    glrow = ebuf[1][0:1, 0:256]
    cdrow = ebuf[1][0:1, 256:512]
    gle = S.op("vector", lambda e: e.tensor_copy(out=ebuf[1][0:H, 768:776], in_=Gnat[:, 127::128]),
               deps=[u8], inc=True)
    cde = S.op("vector", lambda e: e.tensor_copy(out=ebuf[1][0:H, 776:784], in_=egnat[:, 127::128]),
               deps=[u9], inc=True)
    glf0 = S.dma("gpsimd", gl_dram, ebuf[1][0:H, 768:776], deps=[gle, u1], stream="gm")
    cdf0 = S.dma("gpsimd", cd_dram, ebuf[1][0:H, 776:784], deps=[cde], stream="gm")
    glf = S.dma("gpsimd", glrow.rearrange("o (n h) -> o n h", n=NC),
                gl_dram.rearrange("h n -> n h").unsqueeze(0), deps=[glf0], slow=True, stream="gm")
    cdf = S.dma("gpsimd", cdrow.rearrange("o (n h) -> o n h", n=NC),
                cd_dram.rearrange("h n -> n h").unsqueeze(0), deps=[cdf0], slow=True, stream="gm")
    if STOP_AFTER == "dt3b":
        S.barrier(); finish(); return nc
    gl1 = S.op("tensor", lambda e: e.matmul(pA[:, 512:768], onesr, glrow, start=True, stop=True),
               deps=[glf, e2], inc=True)
    cd1 = S.op("tensor", lambda e: e.matmul(pA[:, 768:1024], onesr, cdrow, start=True, stop=True),
               deps=[cdf], inc=True)
    if STOP_AFTER == "dt3c":
        S.barrier(); finish(); return nc
    e4 = S.op("vector", lambda e: e.tensor_tensor(out=ET[:], in0=pA[:, 512:768], in1=ebuf[1][:, 512:768],
                                                  op=ALU.subtract), deps=[gl1, cd1, e2], inc=True)
    e5 = S.op("scalar", lambda e: e.activation(out=ET[:], in_=ET[:], func=AF.Exp), deps=[e4], inc=True)
    if STOP_AFTER == "dt3d":
        S.barrier(); finish(); return nc
    e5b = S.op("vector", lambda e: e.tensor_tensor(out=ET[:], in0=ET[:], in1=fv[:], op=ALU.mult),
               deps=[e5, e3], inc=True)
    e6 = S.op("vector", lambda e: e.tensor_copy(out=cdb[:], in_=pA[:, 768:1024]), deps=[cd1], inc=True)
    if STOP_AFTER == "dt4":
        S.barrier(); finish(); return nc
    gh2 = S.dma("gpsimd", glhs[1:33, :], Gnat, deps=[u8], stream="gm")
    gm_all = ("gm", S.counts["gm"])
    gf = egf = glf = cdf = gh2 = gm_all
    S.barrier()
    if STOP_AFTER == "p1a":
        finish()
        return nc

    slab_cons = [None, None]

    proj_hist = {id(pA): [None, None], id(pB): [None, None]}
    rope_last = [None]

    def project_T(w, n_otiles, put, use_pB=False, wb=None):
        ps = pB if use_pB else pA
        res = []
        ev_hist = proj_hist[id(ps)]
        for ot in range(n_otiles):
            b = ot % 2
            half = ot % 2
            buf = (wb or wbuf)[b]
            wld = S.dma("gpsimd", slab3(buf[:, 0:4096]), colslab(w, ot), deps=[slab_cons[b]],
                        stream=("wb%d" % b) if (wb is None) else ("wc%d" % b))
            mm = None
            for i in range(ND):
                for th in range(2):
                    mm = S.op("tensor", lambda e, buf=buf, i=i, th=th, ps=ps, half=half: e.matmul(
                        ps[:, half * 1024 + th * 512:half * 1024 + (th + 1) * 512],
                        buf[:, i * 128:(i + 1) * 128], xc1s(i, th),
                        start=(i == 0), stop=(i == ND - 1)),
                        deps=([wld, ev_hist[half]] if i == 0 else []), inc=(i == ND - 1 and th == 1))
            slab_cons[b] = mm
            tok = put(ot, mm, ps[:, half * 1024:(half + 1) * 1024])
            res.append(tok)
            ev_hist[half] = tok
        return res

    def rope_T(dst, dep0, scratch_bf):
        lo, hi = dst[0:64, :], dst[64:128, :]
        w0 = scratch_bf
        rot, t1, t2 = w0[:, 0:1024], w0[:, 1024:2048], w0[:, 2048:3072]
        a1 = S.op("vector", lambda e: e.tensor_scalar(out=rot[0:64, :], in0=hi, scalar1=-1.0,
                                                      scalar2=None, op0=ALU.mult),
                  deps=[dep0, td5, td6], inc=True)
        a2 = S.op("vector", lambda e: e.tensor_scalar(out=rot[64:128, :], in0=lo, scalar1=1.0,
                                                      scalar2=None, op0=ALU.mult), inc=True)
        a3 = S.op("vector", lambda e: e.tensor_tensor(out=t1, in0=dst, in1=cosD, op=ALU.mult),
                  deps=[a1], inc=True)
        a4 = S.op("vector", lambda e: e.tensor_tensor(out=t2, in0=rot, in1=sinD, op=ALU.mult),
                  deps=[a2], inc=True)
        a5 = S.op("vector", lambda e: e.tensor_tensor(out=dst, in0=t1, in1=t2, op=ALU.add),
                  deps=[a3, a4], inc=True)
        return a5, a5

    def k_put(ot, mm, psv):
        tgt = kT[:, ot * T:(ot + 1) * T]
        ev = S.op("scalar", lambda e, tgt=tgt, psv=psv: e.activation(out=tgt, in_=psv, func=AF.Copy),
                  deps=[mm], inc=True)
        rope_last[0] = rope_T(tgt, ev, ebuf[0][:].bitcast(BF16))[0]
        return ev

    def v_put(ot, mm, psv):
        tgt = vnat
        evb = ebuf[0][:].bitcast(BF16)[:, 0:1024]
        ev = S.op("scalar", lambda e, evb=evb, psv=psv: e.activation(out=evb, in_=psv, func=AF.Copy),
                  deps=[mm, rope_last[0]], inc=True)
        for n in range(NC):
            ttk = S.op("tensor", lambda e, n=n, evb=evb: e.transpose(
                pBb[:, 2048 + (n % 8) * 128:2048 + ((n % 8) + 1) * 128],
                evb[:, n * 128:(n + 1) * 128], identb[:]),
                deps=([ev, u1, dt_done] if n == 0 else []), inc=(n == NC - 1))
        evs = S.op("vector", lambda e, ot=ot: e.tensor_copy(
            out=vnat.rearrange("p (n kv d) -> p n kv d", n=NC, kv=KV)[:, :, ot, :],
            in_=pBb[:, 2048:3072].rearrange("p (n d) -> p n d", n=NC)), deps=[ttk], inc=True)
        S.wait("tensor", evs)
        S.wait("scalar", evs)
        return ev

    k_res = project_T(wkt, KV, k_put)
    v_res = project_T(wvt, KV, v_put)
    S.barrier()
    if STOP_AFTER == "p1b":
        finish()
        return nc

    def q_put(ot, mm, psv):
        dst = ebuf[0][:].bitcast(BF16)[:, (ot % 2) * 1024:(ot % 2) * 1024 + 1024]
        ev = S.op("scalar", lambda e, dst=dst, psv=psv: e.activation(out=dst, in_=psv, func=AF.Copy),
                  deps=[mm] + ([qsp_hist[ot % 2]] if qsp_hist[ot % 2] else []), inc=True)
        a5, a6 = rope_T(dst, ev, ebuf[1][:].bitcast(BF16))
        rope_last[0] = a5
        sp = S.dma("scalar", q_spill.rearrange("(a p) t -> a p t", p=128)[ot], dst, deps=[a5],
                   stream="qo")
        qsp_hist[ot % 2] = sp
        return ev

    qsp_hist = [None, None]
    q_res = project_T(wqt, ND, q_put, use_pB=True, wb=wbuf2)
    q_sp_last = ("qo", S.counts["qo"])

    def g_put(ot, mm, psv):
        slb = ebuf[1][:].bitcast(BF16)[:, (ot % 2) * 1024:(ot % 2) * 1024 + 1024]
        sl = S.op("scalar", lambda e, slb=slb, psv=psv: e.activation(out=slb, in_=psv, func=AF.Silu),
                  deps=[mm, rope_last[0]], inc=True)
        gout = ebuf[0][:].bitcast(BF16)[:, (ot % 2) * 1024:(ot % 2) * 1024 + 1024]
        ev = S.op("vector", lambda e, slb=slb, gout=gout: e.tensor_scalar(
            out=gout, in0=slb, scalar1=smallc[:, 2:3], scalar2=None, op0=ALU.mult),
            deps=[sl, c_gnw] + ([gsp_hist[ot % 2]] if gsp_hist[ot % 2] else []), inc=True)
        sp = S.dma("scalar", sg_spill.rearrange("(a p) t -> a p t", p=128)[ot],
                   gout, deps=[ev], stream="go")
        gsp_hist[ot % 2] = sp
        return sl

    gsp_hist = list(qsp_hist)
    g_res = project_T(wgt, ND, g_put, use_pB=True, wb=wbuf2)
    g_sp_last = ("go", S.counts["go"])
    S.barrier()
    if STOP_AFTER == "p1c":
        finish()
        return nc

    zst = S.op("gpsimd", lambda e: e.memset(state[:], 0.0), inc=True)
    cs_sp = []
    scan_tok = zst
    ce_prev = None
    for n in range(NC):
        for kv in range(KV):
            ttk = S.op("tensor", lambda e, kv=kv, n=n: e.transpose(
                pAb[:, 1024 + kv * 128:1024 + (kv + 1) * 128],
                kT[:, kv * T + n * 128:kv * T + (n + 1) * 128], identb[:]),
                deps=([scan_tok, e5b, ce_prev, zst] if kv == 0 else []), inc=(kv == KV - 1))
        kd = S.op("vector", lambda e, n=n: e.tensor_tensor(
            out=kdec.rearrange("p (kv r d) -> p kv r d", kv=KV, r=4),
            in0=pAb[:, 1024:2048].rearrange("p (kv d) -> p kv d", kv=KV)
                .unsqueeze(2).broadcast_to([128, KV, 4, HD]),
            in1=ET[:, n * H:(n + 1) * H].rearrange("p (kv r) -> p kv r", kv=KV)
                .unsqueeze(3).broadcast_to([128, KV, 4, HD]),
            op=ALU.mult), deps=[ttk], inc=True)
        S.wait("tensor", kd)
        cs_toks = []
        for h in range(H):
            ps = pA if h < 16 else pB
            off = (h % 16) * 128
            tm = S.op("tensor", lambda e, h=h, ps=ps, off=off, n=n: e.matmul(
                ps[:, off:off + 128], kdec[:, h * 128:(h + 1) * 128],
                vnat[:, n * 1024 + (h // 4) * 128:n * 1024 + (h // 4 + 1) * 128],
                start=True, stop=True), deps=([kd] if h == 0 else []), inc=(h in (15, 31)))
            if h in (15, 31):
                cs_toks.append(tm)
        ce1 = S.op("scalar", lambda e: e.activation(out=csbuf[:, 0:2048], in_=pA[:], func=AF.Copy),
                   deps=[cs_toks[0]] + ([cs_sp[-1]] if cs_sp else []), inc=True)
        ce2 = S.op("scalar", lambda e: e.activation(out=csbuf[:, 2048:4096], in_=pB[:], func=AF.Copy),
                   deps=[cs_toks[1]], inc=True)
        ce_prev = ce2
        csp = S.dma("scalar", cs_spill[n], csbuf[:], deps=[ce1, ce2], stream="co")
        cs_sp.append(csp)
        s1 = S.op("vector", lambda e, n=n: e.tensor_tensor(
            out=state[:].rearrange("p (h d) -> p h d", h=H),
            in0=state[:].rearrange("p (h d) -> p h d", h=H),
            in1=cdb[:, n * H:(n + 1) * H].unsqueeze(2).broadcast_to([128, H, HD]),
            op=ALU.mult), deps=[e6, kd], inc=True)
        s2 = S.op("vector", lambda e: e.tensor_tensor(out=state[:, 0:2048], in0=state[:, 0:2048],
                                                      in1=pA[:], op=ALU.add), deps=[s1, cs_toks[0]], inc=True)
        s3 = S.op("vector", lambda e: e.tensor_tensor(out=state[:, 2048:4096], in0=state[:, 2048:4096],
                                                      in1=pB[:], op=ALU.add), deps=[s1, cs_toks[1]], inc=True)
        scan_tok = s3
        S.wait("tensor", s2)
        S.wait("tensor", s3)
    st_out = S.dma("gpsimd", cc_in.ap().rearrange("(h d) e -> d h e", h=H),
                   state[:].rearrange("d (h e) -> d h e", h=H), deps=[scan_tok], stream="gm")
    S.waits("gpsimd", [st_out])
    S.counts["sC"] += 1
    cc_tok = ("sC", S.counts["sC"])

    def cc_fn(e):
        return e.collective_compute("AllGather", ALU.bypass, replica_groups=replica_groups,
                                    ins=[cc_in.ap().opt()], outs=[cc_out.ap().opt()])
    S.items["gpsimd"].append(["op", cc_fn, "sC"])
    tld = S.dma("gpsimd", state[:].rearrange("d (h e) -> d h e", h=H),
                cc_out.ap()[0:H * HD, :].rearrange("(h d) e -> d h e", h=H), deps=[cc_tok], stream="gm")
    tc0 = S.op("vector", lambda e: e.tensor_scalar(out=state[:], in0=state[:], scalar1=smallc[:, 3:4],
                                                   scalar2=None, op0=ALU.mult), deps=[tld, c_ccol], inc=True)
    tb0 = S.op("scalar", lambda e: e.activation(out=state_bf[:], in_=state[:], func=AF.Copy),
               deps=[tc0], inc=True)
    S.barrier()
    if STOP_AFTER == "p2":
        finish()
        return nc

    c_negb = S.dma("sync", grhs[1:33, :], negblock, stream="cn")
    qsp = q_spill.rearrange("(a p) t -> p a t", p=128)

    def load_q(n, deps):
        tok = None
        for gidx in range(8):
            tok = S.dma("sync", qbuf[:, gidx * 512:(gidx + 1) * 512].rearrange("p (a t) -> p a t", a=4),
                        qsp[:, gidx * 4:(gidx + 1) * 4, n * 128:(n + 1) * 128],
                        deps=deps if gidx == 0 else [], stream="qb")
        return tok

    qld = load_q(0, [q_sp_last])
    csld = S.dma("sync", csbuf[:], cs_spill[0], deps=[cs_sp[-1]], stream="cl")
    grl = S.dma("gpsimd", grhs[0:1, :], gflat_d[:, 0:4096], deps=[gf, c_negb], stream="gr")
    st_up_prev = tc0
    sbf_prev = tb0
    pa_free = None
    pb_free = None
    egl_prev = None
    for n in range(NC):
        qld_n, csld_n, grl_n = qld, csld, grl
        ymm_b = None
        dmm = None
        for hf in range(2):
            egl = S.dma("gpsimd", egrow, egflat_d[:, n * 4096 + hf * 2048:n * 4096 + (hf + 1) * 2048],
                        deps=[egf, egl_prev], stream="eg")
            dmm = None
            for sl in range(4):
                dmm = S.op("tensor", lambda e, sl=sl, hf=hf, n=n: e.matmul(
                    pA[:, sl * 512:(sl + 1) * 512], glhs[:, n * 128:(n + 1) * 128],
                    grhs[:, hf * 2048 + sl * 512:hf * 2048 + (sl + 1) * 512],
                    start=True, stop=False),
                    deps=([grl_n, gh1, gh2, pa_free] if sl == 0 else []))
            for sl in range(4):
                dmm = S.op("tensor", lambda e, sl=sl: e.matmul(
                    pA[:, sl * 512:(sl + 1) * 512], slT[:],
                    negidS[:].unsqueeze(1).broadcast_to([128, 4, 128]),
                    start=False, stop=True), deps=([c_sl, c_negid] if sl == 0 else []), inc=(sl == 3))
            dc = S.op("scalar", lambda e, hf=hf: e.activation(out=decayT[:, hf * 2048:(hf + 1) * 2048],
                                                              in_=pA[:], func=AF.Exp), deps=[dmm], inc=True)
            if DEBUG_SPILL and n == 1:
                S.dma("scalar", dbg_dc[:, hf * 2048:(hf + 1) * 2048], decayT[:, hf * 2048:(hf + 1) * 2048],
                      deps=[dc], stream="ro")
            smm = None
            for h in range(16):
                hh = hf * 16 + h
                smm = S.op("tensor", lambda e, hh=hh, h=h, n=n: e.matmul(
                    pA[:, h * 128:(h + 1) * 128], kT[:, (hh // 4) * T + n * 128:(hh // 4) * T + (n + 1) * 128],
                    qbuf[:, hh * 128:(hh + 1) * 128], start=True, stop=True),
                    deps=([dc, qld_n] if h == 0 else []), inc=(h == 15))
            ap_ = S.op("vector", lambda e, hf=hf: e.tensor_tensor(
                out=apT[:, hf * 2048:(hf + 1) * 2048], in0=pA[:], in1=decayT[:, hf * 2048:(hf + 1) * 2048],
                op=ALU.mult), deps=[smm, dc], inc=True)
            ap2 = S.op("vector", lambda e, hf=hf, n=n: e.tensor_tensor(
                out=apT[:, hf * 2048:(hf + 1) * 2048].rearrange("p (h i) -> p h i", h=16),
                in0=apT[:, hf * 2048:(hf + 1) * 2048].rearrange("p (h i) -> p h i", h=16),
                in1=fv[:, n * H + hf * 16:n * H + (hf + 1) * 16].unsqueeze(2).broadcast_to([128, 16, 128]),
                op=ALU.mult), deps=[ap_, e3], inc=True)
            if DEBUG_SPILL and n == 1:
                S.dma("scalar", dbg_ap[:, hf * 2048:(hf + 1) * 2048], apT[:, hf * 2048:(hf + 1) * 2048],
                      deps=[ap2], stream="ro")
            ymm_a = None
            for h in range(16):
                hh = hf * 16 + h
                ymm_a = S.op("tensor", lambda e, hh=hh, h=h, n=n: e.matmul(
                    pA[:, h * 128:(h + 1) * 128],
                    vnat[:, n * 1024 + (hh // 4) * 128:n * 1024 + (hh // 4 + 1) * 128],
                    apT[:, hh * 128:(hh + 1) * 128], start=True, stop=True),
                    deps=([ap2] if h == 0 else []), inc=(h == 15))
            for h in range(16):
                hh = hf * 16 + h
                ymm_b = S.op("tensor", lambda e, hh=hh, h=h: e.matmul(
                    pB[:, h * 128:(h + 1) * 128], state_bf[:, hh * 128:(hh + 1) * 128],
                    qbuf[:, hh * 128:(hh + 1) * 128], start=True, stop=True),
                    deps=([sbf_prev, pb_free] if h == 0 else []), inc=(h == 15))
            yi = S.op("scalar", lambda e: e.activation(out=ebuf[0][:].bitcast(BF16)[:, 0:2048], in_=pB[:],
                                                       func=AF.Copy), deps=[ymm_b], inc=True)
            if DEBUG_SPILL and n == 1:
                S.dma("scalar", dbg_yi[:, hf * 2048:(hf + 1) * 2048], ebuf[0][:].bitcast(BF16)[:, 0:2048],
                      deps=[yi], stream="ro")
            ebmm = None
            for sl in range(4):
                ebmm = S.op("tensor", lambda e, sl=sl: e.matmul(
                    pB[:, sl * 512:(sl + 1) * 512], onesb[0:1, :],
                    egrow[:, sl * 512:(sl + 1) * 512], start=True, stop=True),
                    deps=([yi, egl] if sl == 0 else []), inc=(sl == 3))
            egl_prev = ebmm
            y1 = S.op("vector", lambda e: e.tensor_tensor(out=ebuf[1][:, 0:2048],
                                                          in0=ebuf[0][:].bitcast(BF16)[:, 0:2048],
                                                          in1=pB[:], op=ALU.mult), deps=[ebmm, yi], inc=True)
            y2 = S.op("vector", lambda e: e.tensor_tensor(out=ebuf[1][:, 0:2048], in0=ebuf[1][:, 0:2048],
                                                          in1=pA[:], op=ALU.add), deps=[y1, ymm_a], inc=True)
            pa_free = y2
            sq = S.op("vector", lambda e: e.tensor_tensor(out=ebuf[0][:].bitcast(BF16)[:, 2048:4096],
                                                          in0=ebuf[1][:, 0:2048], in1=ebuf[1][:, 0:2048],
                                                          op=ALU.mult), deps=[y2], inc=True)
            rmm = None
            for sl in range(4):
                rmm = S.op("tensor", lambda e, sl=sl: e.matmul(
                    pB[:, sl * 512:(sl + 1) * 512], onesb[:],
                    ebuf[0][:].bitcast(BF16)[:, 2048 + sl * 512:2048 + (sl + 1) * 512],
                    start=True, stop=True), deps=([sq, y1] if sl == 0 else []), inc=(sl == 3))
            r1 = S.op("scalar", lambda e: e.activation(out=ebuf[0][:, 0:2048], in_=pB[:], func=AF.Ln,
                                                       scale=1.0 / HD, bias=RMS_EPS), deps=[rmm, yi, c_cv], inc=True)
            r2 = S.op("scalar", lambda e: e.activation(out=ebuf[0][:, 0:2048], in_=ebuf[0][:, 0:2048],
                                                       func=AF.Exp, scale=-0.5), deps=[r1], inc=True)
            pb_free = r1
            yt = S.op("vector", lambda e, n=n, hf=hf: e.tensor_tensor(
                out=ynT.rearrange("p (h nn i) -> p h nn i", h=H, nn=NC)[:, hf * 16:(hf + 1) * 16, n, :],
                in0=ebuf[1][:, 0:2048].rearrange("p (h i) -> p h i", h=16),
                in1=ebuf[0][:, 0:2048].rearrange("p (h i) -> p h i", h=16), op=ALU.mult),
                deps=[y2, r2], inc=True)
        if n + 1 < NC:
            qld = load_q(n + 1, [ymm_b])
            grl = S.dma("gpsimd", grhs[0:1, :], gflat_d[:, (n + 1) * 4096:(n + 2) * 4096], deps=[dmm],
                        stream="gr")
        su1 = S.op("vector", lambda e, n=n: e.tensor_tensor(
            out=state[:].rearrange("p (h d) -> p h d", h=H),
            in0=state[:].rearrange("p (h d) -> p h d", h=H),
            in1=cdb[:, n * H:(n + 1) * H].unsqueeze(2).broadcast_to([128, H, HD]),
            op=ALU.mult), deps=[e6, sbf_prev, st_up_prev], inc=True)
        su2 = S.op("vector", lambda e: e.tensor_tensor(out=state[:], in0=state[:], in1=csbuf[:],
                                                       op=ALU.add), deps=[su1, csld_n], inc=True)
        st_up_prev = su2
        if n + 1 < NC:
            csld = S.dma("sync", csbuf[:], cs_spill[n + 1], deps=[su2], stream="cl")
            sbf = S.op("scalar", lambda e: e.activation(out=state_bf[:], in_=state[:], func=AF.Copy),
                       deps=[su2, ymm_b], inc=True)
            sbf_prev = sbf
    if DEBUG_SPILL:
        S.dma("scalar", dbg_yn, ynT, deps=[ydone_prev] if False else [yt], stream="ro")
    S.barrier()
    if STOP_AFTER == "p3":
        finish()
        return nc

    sgsp = sg_spill.rearrange("(a p) t -> p a t", p=128)
    sg_cons = [None, None]
    ygt = None
    for idx in range(16):
        th, hg = idx % 2, idx // 2
        slot = idx % 2
        sgl = S.dma("sync", sgbuf[:, slot * 2048:(slot + 1) * 2048].rearrange("p (a t) -> p a t", a=4),
                    sgsp[:, hg * 4:(hg + 1) * 4, th * 512:(th + 1) * 512],
                    deps=[g_sp_last, sg_cons[slot]], stream="sg%d" % slot)
        ygt = S.op("vector", lambda e, th=th, hg=hg, slot=slot: e.tensor_tensor(
            out=ynT.rearrange("p (h t) -> p h t", h=H)[:, hg * 4:(hg + 1) * 4, th * 512:(th + 1) * 512],
            in0=ynT.rearrange("p (h t) -> p h t", h=H)[:, hg * 4:(hg + 1) * 4, th * 512:(th + 1) * 512],
            in1=sgbuf[:, slot * 2048:(slot + 1) * 2048].rearrange("p (a t) -> p a t", a=4),
            op=ALU.mult), deps=[sgl], inc=True)
        sg_cons[slot] = ygt
    gate_all = ygt

    c_fs = S.dma("sync", smallc[:, 5:9] if False else f_all[:, 0:128], wq_fs, stream="c5")
    c_zs = S.dma("sync", f_all[:, 128:256], wq_zs, stream="c5")
    c_s2s = S.dma("sync", f_all[:, 256:260], wq_s2s, stream="c5")
    c5b = ("c5", S.counts["c5"])
    ff1 = S.op("vector", lambda e: e.tensor_tensor(
        out=f_all[:, 0:128].rearrange("p (j g) -> p j g", j=4),
        in0=f_all[:, 0:128].rearrange("p (j g) -> p j g", j=4),
        in1=f_all[:, 256:260].unsqueeze(2).broadcast_to([128, 4, 32]),
        op=ALU.mult), deps=[c5b], inc=True)
    qslab = big2.ap()[:, 0:4096]
    mslab = big2.ap()[:, 4096:8192]
    dqs = big3.ap().bitcast(F32)
    wts = big6.ap()
    _ffn_state = {"fo": None, "dq": None}

    def emit_ffn_split_j(j):
        fql = S.dma("gpsimd", qslab, wq_qs[j], deps=[_ffn_state["dq"]], stream="fq")
        fml = S.dma("gpsimd", mslab, wq_ms[j], deps=[_ffn_state["dq"]], stream="fm")
        fd1 = S.op("vector", lambda e, j=j: e.tensor_tensor(
            out=dqs.rearrange("p (g i) -> p g i", g=32),
            in0=qslab.rearrange("p (g i) -> p g i", g=32),
            in1=f_all[:, 128 + j * 32:128 + (j + 1) * 32].unsqueeze(2).broadcast_to([128, 32, 128]),
            op=ALU.subtract), deps=[fql, ff1], inc=True)
        fd2 = S.op("vector", lambda e, j=j: e.tensor_tensor(
            out=dqs.rearrange("p (g i) -> p g i", g=32),
            in0=dqs.rearrange("p (g i) -> p g i", g=32),
            in1=f_all[:, j * 32:(j + 1) * 32].unsqueeze(2).broadcast_to([128, 32, 128]),
            op=ALU.mult), deps=[fd1], inc=True)
        fd3 = S.op("vector", lambda e: e.tensor_tensor(out=qslab, in0=dqs, in1=mslab,
                                                       op=ALU.mult), deps=[fd2, fml], inc=True)
        tev3 = []
        for nb in range(32):
            fttk = S.op("tensor", lambda e, nb=nb: e.transpose(
                pAb[:, 2048 + (nb % 4) * 128:2048 + ((nb % 4) + 1) * 128],
                qslab[:, nb * 128:(nb + 1) * 128], identb[:]),
                deps=([fd3, _ffn_state["fo"]] if nb == 0 else []), inc=(nb % 4 == 3))
            if nb % 4 == 3:
                fev = S.op("vector", lambda e, nb=nb: e.tensor_copy(
                    out=wts[:, (nb - 3) * 128:(nb + 1) * 128], in_=pAb[:, 2048:2560]),
                    deps=[fttk], inc=True)
                tev3.append(fev)
                S.wait("tensor", fev)
        _ffn_state["dq"] = fd3
        _ffn_state["fo"] = S.dma("gpsimd", wdq_in.rearrange("(j p) n -> j p n", p=128)[j], wts[:],
                                 deps=tev3, stream="fo")


    wo_cons = [None, None]
    xt_cons = [None, None]
    xsp_cons = [None, None]
    st2 = None
    stat_cons = None
    x2_last = None
    for ot in range(ND):
        bb = ot % 2
        if ot >= 4 and ot % 4 == 0 and (ot - 4) // 4 < 4:
            emit_ffn_split_j((ot - 4) // 4)
        wld = S.dma("gpsimd", slab3(wbuf[bb][:, 0:4096]), colslab(wot, ot), deps=[wo_cons[bb], tb0],
                    stream="wb%d" % bb)
        mm = None
        for th in range(2):
            for hd in range(ND):
                mm = S.op("tensor", lambda e, bb=bb, th=th, hd=hd: e.matmul(
                    pA[:, th * 512:(th + 1) * 512], wbuf[bb][:, hd * 128:(hd + 1) * 128],
                    ynTv(hd, th * 512, (th + 1) * 512), start=(hd == 0), stop=(hd == ND - 1)),
                    deps=([wld, gate_all] if (th == 0 and hd == 0) else []),
                    inc=(th == 1 and hd == ND - 1))
        wo_cons[bb] = mm
        xl = S.dma("sync", xtile[bb][:], xTt[ot], deps=[xt_cons[bb], xsp_cons[bb]], stream="xt%d" % bb)
        x2v = S.op("vector", lambda e, bb=bb: e.tensor_tensor(out=xtile[bb][:], in0=xtile[bb][:],
                                                              in1=pA[:, 0:1024], op=ALU.add),
                   deps=[xl, mm], inc=True)
        S.wait("tensor", x2v)
        xsp = S.dma("scalar", x2t[ot], xtile[bb][:], deps=[x2v], stream="x2o")
        xsp_cons[bb] = xsp
        x2_last = xsp
        cb = S.op("vector", lambda e, bb=bb: e.tensor_copy(out=ebuf[0][:].bitcast(BF16)[:, 0:1024],
                                                           in_=xtile[bb][:]), deps=[x2v, stat_cons], inc=True)
        cs = S.op("scalar", lambda e, bb=bb: e.activation(out=ebuf[0][:].bitcast(BF16)[:, 1024:2048],
                                                          in_=xtile[bb][:], func=AF.Square),
                  deps=[x2v, stat_cons, xsp], inc=True)
        xt_cons[bb] = cs
        for th in range(2):
            S.op("tensor", lambda e, th=th, ot=ot: e.matmul(
                pB[0:1, th * 512:(th + 1) * 512], onesb[:, 0:1],
                ebuf[0][:].bitcast(BF16)[:, th * 512:(th + 1) * 512],
                start=(ot == 0), stop=(ot == ND - 1)), deps=[cb])
            st2 = S.op("tensor", lambda e, th=th, ot=ot: e.matmul(
                pB[0:1, 1024 + th * 512:1024 + (th + 1) * 512], onesb[:, 0:1],
                ebuf[0][:].bitcast(BF16)[:, 1024 + th * 512:1024 + (th + 1) * 512],
                start=(ot == 0), stop=(ot == ND - 1)), deps=[cs], inc=True)
        stat_cons = st2
    S.waits("gpsimd", [_ffn_state["fo"]])
    S.counts["sC"] += 1
    cc2_tok = ("sC", S.counts["sC"])

    def cc2_fn(e):
        return e.collective_compute("AllGather", ALU.bypass, replica_groups=[list(range(8))],
                                    ins=[wdq_in.ap().opt()], outs=[wdq_out.ap().opt()])
    S.items["gpsimd"].append(["op", cc2_fn, "sC"])
    S.barrier()

    mu2 = ebuf[1][0:1, 0:1024]
    r2r = ebuf[1][0:1, 1024:2048]
    w1 = ebuf[0][0:1, 0:1024]
    w2 = ebuf[0][0:1, 1024:2048]
    m1 = S.op("vector", lambda e: e.tensor_scalar(out=mu2, in0=pB[0:1, 0:1024], scalar1=1.0 / D,
                                                  scalar2=None, op0=ALU.mult), deps=[st2], inc=True)
    m2 = S.op("vector", lambda e: e.tensor_scalar(out=w1, in0=pB[0:1, 1024:2048], scalar1=1.0 / D,
                                                  scalar2=None, op0=ALU.mult), deps=[st2], inc=True)
    m3 = S.op("vector", lambda e: e.scalar_tensor_tensor(out=w2, in0=mu2, scalar=-1.0, in1=mu2,
                                                         op0=ALU.mult, op1=ALU.mult), deps=[m1], inc=True)
    m4 = S.op("vector", lambda e: e.tensor_tensor(out=w1, in0=w1, in1=w2, op=ALU.add), deps=[m2, m3], inc=True)
    m5 = S.op("scalar", lambda e: e.activation(out=w2, in_=w1, func=AF.Ln, bias=LN_EPS), deps=[m4, c_cv], inc=True)
    m6 = S.op("scalar", lambda e: e.activation(out=r2r, in_=w2, func=AF.Exp, scale=-0.5), deps=[m5], inc=True)

    mo2 = None
    for th in range(2):
        mo2 = S.op("tensor", lambda e, th=th: e.matmul(pB[:, th * 512:(th + 1) * 512], onesr,
                                                       mu2[:, th * 512:(th + 1) * 512], start=True, stop=True),
                   deps=[m1, m6, st2], inc=(th == 1))
    ro2 = None
    for th in range(2):
        ro2 = S.op("tensor", lambda e, th=th: e.matmul(pB[:, 1024 + th * 512:1024 + (th + 1) * 512],
                                                       onesr, r2r[:, th * 512:(th + 1) * 512],
                                                       start=True, stop=True), deps=[m6, m2], inc=(th == 1))
    rbe = S.op("vector", lambda e: e.tensor_copy(out=rbr[:], in_=pB[:, 1024:2048]), deps=[ro2], inc=True)

    xc2_all = None
    for i in range(ND):
        bb = i % 2
        xl = S.dma("sync", xtile[bb][:], x2t[i], deps=[x2_last, xt_cons[bb]], stream="xt%d" % bb)
        tv = S.op("vector", lambda e, bb=bb: e.tensor_tensor(out=xtile[bb][:], in0=xtile[bb][:],
                                                             in1=pB[:, 0:1024], op=ALU.subtract),
                  deps=[xl, mo2, gate_all], inc=True)
        tv2 = S.op("vector", lambda e, bb=bb, i=i: e.tensor_tensor(out=xc1s(i), in0=xtile[bb][:],
                                                                   in1=rbr[:], op=ALU.mult),
                   deps=[tv, rbe], inc=True)
        xt_cons[bb] = tv2
        xc2_all = tv2
    S.barrier()
    if STOP_AFTER == "p4":
        finish()
        return nc

    biasrow = big8.ap()[0:1, 4096:8192]
    c_bias = S.dma("gpsimd", biasrow, wq_b, stream="cb")
    cb_all = ("cb", S.counts["cb"])
    c_bias = cb_all
    wq_cons = [None, None]
    out_toks = []
    outr2 = wdq_out.rearrange("(kt p) n -> kt p n", p=128)
    o2_cons = [None, None]
    o1_tok = None
    for kt in range(32):
        bb = kt % 2
        wl5 = S.dma("sync", wbuf[bb][:], outr2[kt], deps=[wq_cons[bb], cc2_tok], stream="w5%d" % bb)
        xl = S.dma("sync", xtile[bb][:], x2t[kt], deps=[o2_cons[bb]], stream="xt%d" % bb)
        for th in range(2):
            mm = None
            for nb in range(32):
                mm = S.op("tensor", lambda e, th=th, nb=nb, bb=bb: e.matmul(
                    pB[:, th * 512:(th + 1) * 512], wbuf[bb][:, nb * 128:(nb + 1) * 128],
                    xc1s(nb, th), start=(nb == 0), stop=False),
                    deps=([wl5, o1_tok, xc2_all] if nb == 0 else []))
            mm = S.op("tensor", lambda e, th=th, kt=kt: e.matmul(
                pB[:, th * 512:(th + 1) * 512], biasrow[0:1, kt * 128:(kt + 1) * 128],
                onesb[0:1, :].unsqueeze(1).broadcast_to([1, 4, 128]),
                start=False, stop=True), deps=[c_bias], inc=True)
            o2 = S.op("vector", lambda e, th=th, bb=bb: e.tensor_tensor(
                out=ebuf[1][:, 0:512], in0=pB[:, th * 512:(th + 1) * 512],
                in1=xtile[bb][:, th * 512:(th + 1) * 512], op=ALU.add),
                deps=[mm, xl] + ([out_toks[-1]] if out_toks else []), inc=True)
            osp = S.dma("scalar", outr[kt][:, th * 512:(th + 1) * 512], ebuf[1][:, 0:512], deps=[o2],
                        stream="oo")
            out_toks.append(osp)
            o1_tok = o2
        wq_cons[bb] = mm
        o2_cons[bb] = o2

    S.wait("scalar", ("oo", S.counts["oo"]))
    S.wait("scalar", ("x2o", S.counts["x2o"]))

    finish()
    return nc


def _resetm():
    r = np.ones((H, T), np.float32)
    r[:, ::CH] = 0.0
    return r


def make_consts():
    return dict(
        eye_bf=np.eye(128, dtype=BF),
        eye_f32=np.tile(np.eye(32, dtype=np.float32), (4, 1)),
        sl_tri=np.triu(np.ones((128, 128), np.float32), 1).astype(BF),
        negid=(NEG * np.eye(128, dtype=np.float32)).astype(BF),
        negblock=np.repeat(-np.eye(32, dtype=np.float32), 128, axis=1),
        resetm=_resetm(),
        onesbf=np.ones((128, 128), dtype=BF),
        constv=np.tile(np.array([[LN_EPS, 1.0, -PI]], np.float32), (128, 1)),
    )


def make_in_maps(inputs, n_cores=8):
    x = np.asarray(inputs["L0_attn_hidden_states"])

    def TP(a):
        return np.ascontiguousarray(np.asarray(a).T)

    consts = make_consts()
    shared = {
        "wqt": TP(inputs["L0_attn_Wq"]), "wkt": TP(inputs["L0_attn_Wk"]),
        "wvt": TP(inputs["L0_attn_Wv"]), "wgt": TP(inputs["L0_attn_Wg"]),
        "wot": TP(inputs["L0_attn_Wo"]), "wint": TP(inputs["L0_attn_Win"]),
        "dtb": np.ascontiguousarray(np.asarray(inputs["L0_attn_dt_bias"], np.float32).reshape(H, 1)),
        "alog": np.ascontiguousarray(np.asarray(inputs["L0_attn_A_log_bias"], np.float32).reshape(H, 1)),
        "gnw": np.ascontiguousarray(np.asarray(inputs["L0_attn_gnorm_w"], np.float32).reshape(HD, 1)),
        "wq_b": np.ascontiguousarray(np.asarray(inputs["L0_ffn_bias"], np.float32).reshape(1, D)),
    }
    shared.update(consts)
    in_maps = []
    for c in range(n_cores):
        b, half = c // 2, c % 2
        m = dict(shared)
        m["xT"] = np.ascontiguousarray(x[b, half * T:(half + 1) * T, :].T)
        ks = slice(512 * c, 512 * (c + 1))
        m["wq_qs"] = np.ascontiguousarray(np.asarray(inputs["L0_ffn_W_q"], np.int32)[ks].reshape(4, 128, D))
        m["wq_ms"] = np.ascontiguousarray(np.asarray(inputs["L0_ffn_mask"], np.int32)[ks].reshape(4, 128, D))
        scs = np.asarray(inputs["L0_ffn_scales"], np.float32).reshape(D, 32)[ks].reshape(4, 128, 32)
        zrs = np.asarray(inputs["L0_ffn_zeros"], np.float32).reshape(D, 32)[ks].reshape(4, 128, 32)
        s2s = np.asarray(inputs["L0_ffn_scale2"], np.float32).reshape(D)[ks].reshape(4, 128)
        m["wq_fs"] = np.ascontiguousarray(scs.transpose(1, 0, 2).reshape(128, 128))
        m["wq_zs"] = np.ascontiguousarray(zrs.transpose(1, 0, 2).reshape(128, 128))
        m["wq_s2s"] = np.ascontiguousarray(s2s.T)
        posv = half * T + np.arange(T, dtype=np.float64)
        inv = 1.0 / 10000.0 ** (np.arange(0, HD, 2, dtype=np.float64) / HD)
        ang = posv[:, None] * inv
        m["cos_d"] = np.ascontiguousarray(np.tile(np.cos(ang).T, (2, 1)).astype(BF))
        m["sin_d"] = np.ascontiguousarray(np.tile(np.sin(ang).T, (2, 1)).astype(BF))
        m["ccol"] = np.full((128, 1), float(half), np.float32)
        in_maps.append(m)
    return in_maps


def kernel(**inputs):
    x = np.asarray(inputs["L0_attn_hidden_states"])
    B, L, _ = x.shape
    n_cores = 8
    nc = build_program([[0, 1], [2, 3], [4, 5], [6, 7]])
    in_maps = make_in_maps(inputs, n_cores)
    res = run_bass_kernel_spmd(nc, in_maps, core_ids=list(range(n_cores)))
    outp = np.empty((B, L, D), np.float32)
    for c in range(n_cores):
        b, half = c // 2, c % 2
        outp[b, half * T:(half + 1) * T, :] = res.results[c]["out"].T
    return outp


# version 15
# speedup vs baseline: 1.0577x; 1.0191x over previous
import numpy as np
import ml_dtypes
import concourse.bass as bass
import concourse.mybir as mybir
from concourse.bass_utils import run_bass_kernel_spmd
from contextlib import ExitStack

F32 = mybir.dt.float32
BF16 = mybir.dt.bfloat16
I32 = mybir.dt.int32
AF = mybir.ActivationFunctionType
ALU = mybir.AluOpType
BF = ml_dtypes.bfloat16

D = 4096
T = 1024
NC = 8
CH = 128
H = 32
HD = 128
KV = 8
ND = D // 128
LN_EPS = 1e-5
DEBUG_SPILL = False
STOP_AFTER = "all"
RMS_EPS = 1e-5
NEG = -1e30
PI = float(np.pi)

ENGS = ("sync", "gpsimd", "tensor", "vector", "scalar")
ENG_SEM = {"tensor": "sT", "vector": "sV", "scalar": "sA", "gpsimd": "sP"}
DMA_STREAMS = ("cst", "cp", "cr", "cn", "c5", "xt0", "xt1", "wb0", "wb1", "wc0", "wc1",
               "qb", "qb2", "cl", "cl2", "cl3", "cl4", "gr", "eg", "gm", "sg0", "sg1", "sg2", "sg3", "wo0", "wo1", "x2o", "x2p", "qo", "go", "co", "ro", "oo", "cb", "fq", "fm", "fo", "w50", "w51")
ALL_SEMS = DMA_STREAMS + ("sT", "sV", "sA", "sP", "sC")


class Sched:
    def __init__(self):
        self.items = {k: [] for k in ENGS}
        self.counts = {s: 0 for s in ALL_SEMS}
        self.last_wait = {k: {} for k in ENGS}
        self.sems = {}

    def wait(self, eng, tok):
        if tok is None:
            return
        sem, val = tok
        if val <= 0 or val <= self.last_wait[eng].get(sem, 0):
            return
        self.last_wait[eng][sem] = val
        self.items[eng].append(["wait", (sem, val), None])

    def waits(self, eng, toks):
        for t in toks:
            self.wait(eng, t)

    def dma(self, eng, out_ap, in_ap, deps=(), slow=False, stream="cst"):
        self.waits(eng, deps)
        sem = stream
        self.counts[sem] += 16
        self.items[eng].append(["dma", lambda e, o=out_ap, i=in_ap, s=slow: e.dma_start(
            out=o, in_=i, allow_slow_non_contiguous=s), sem])
        return (sem, self.counts[sem])

    def op(self, eng, fn, deps=(), inc=False):
        self.waits(eng, deps)
        if inc:
            sem = ENG_SEM[eng]
            self.counts[sem] += 1
            self.items[eng].append(["op", fn, sem])
            return (sem, self.counts[sem])
        self.items[eng].append(["op", fn, None])
        return None

    def mark(self, eng):
        sem = ENG_SEM.get(eng)
        for item in reversed(self.items[eng]):
            if item[0] == "op":
                if item[2] is None:
                    self.counts[sem] += 1
                    item[2] = sem
                return (item[2], self.counts[item[2]])
            if item[0] == "dma":
                return (item[2], self.counts[item[2]])
        return None

    def barrier(self):
        toks = [self.mark(e) for e in ("tensor", "vector", "scalar", "gpsimd")]
        toks += [(s, self.counts[s]) for s in DMA_STREAMS if self.counts[s] > 0]
        for eng in ENGS:
            self.waits(eng, toks)

    def replay(self, block, sems):
        self.sems = sems
        for eng in ENGS:
            items = self.items[eng]

            def make(items):
                def body(e):
                    for kind, fn, sem in items:
                        if kind == "wait":
                            s, v = fn
                            e.wait_ge(self.sems[s], v)
                        else:
                            inst = fn(e)
                            if sem is not None:
                                inst.then_inc(self.sems[sem], 16 if kind == "dma" else 1)
                return body

            getattr(block, eng)(make(items))


def build_program(replica_groups):
    nc = bass.Bass("TRN2", target_bir_lowering=False, debug=False)
    S = Sched()
    st = ExitStack()

    def SB(name, shape, dtype):
        return st.enter_context(nc.sbuf_tensor(name, shape, dtype))

    EI = "ExternalInput"
    xT = nc.dram_tensor("xT", [D, T], F32, kind=EI).ap()
    wqt = nc.dram_tensor("wqt", [ND, 128, D], F32, kind=EI).ap()
    wkt = nc.dram_tensor("wkt", [KV, 128, D], F32, kind=EI).ap()
    wvt = nc.dram_tensor("wvt", [KV, 128, D], F32, kind=EI).ap()
    wgt = nc.dram_tensor("wgt", [ND, 128, D], F32, kind=EI).ap()
    wot = nc.dram_tensor("wot", [ND, 128, D], F32, kind=EI).ap()
    wint = nc.dram_tensor("wint", [128, ND * H], F32, kind=EI).ap()
    dtb = nc.dram_tensor("dtb", [H, 1], F32, kind=EI).ap()
    alog = nc.dram_tensor("alog", [H, 1], F32, kind=EI).ap()
    gnw = nc.dram_tensor("gnw", [HD, 1], F32, kind=EI).ap()
    wq_qs = nc.dram_tensor("wq_qs", [4, 128, D], I32, kind=EI).ap()
    wq_ms = nc.dram_tensor("wq_ms", [4, 128, D], I32, kind=EI).ap()
    wq_fs = nc.dram_tensor("wq_fs", [128, 4 * 32], F32, kind=EI).ap()
    wq_zs = nc.dram_tensor("wq_zs", [128, 4 * 32], F32, kind=EI).ap()
    wq_s2s = nc.dram_tensor("wq_s2s", [128, 4], F32, kind=EI).ap()
    wq_b = nc.dram_tensor("wq_b", [1, D], F32, kind=EI).ap()
    cosd_d = nc.dram_tensor("cos_d", [128, T], BF16, kind=EI).ap()
    sind_d = nc.dram_tensor("sin_d", [128, T], BF16, kind=EI).ap()
    ccol_d = nc.dram_tensor("ccol", [128, 1], F32, kind=EI).ap()
    eye_bf = nc.dram_tensor("eye_bf", [128, 128], BF16, kind=EI).ap()
    eye_f32 = nc.dram_tensor("eye_f32", [128, 32], F32, kind=EI).ap()
    sl_tri = nc.dram_tensor("sl_tri", [128, 128], BF16, kind=EI).ap()
    negid_d = nc.dram_tensor("negid", [128, 128], BF16, kind=EI).ap()
    negblock = nc.dram_tensor("negblock", [32, D], F32, kind=EI).ap()
    resetm = nc.dram_tensor("resetm", [H, T], F32, kind=EI).ap()
    onesbf = nc.dram_tensor("onesbf", [128, 128], BF16, kind=EI).ap()
    constv = nc.dram_tensor("constv", [128, 3], F32, kind=EI).ap()

    out = nc.dram_tensor("out", [D, T], F32, kind="ExternalOutput").ap()

    q_spill = nc.dram_tensor("q_spill", [H * HD, T], BF16).ap()
    wot_bf = nc.dram_tensor("wot_bf", [ND, 128, D], BF16).ap()
    sg_spill = nc.dram_tensor("sg_spill", [H * HD, T], BF16).ap()
    cs_spill = nc.dram_tensor("cs_spill", [NC, 128, H * HD], BF16).ap()
    x2_hbm = nc.dram_tensor("x2_hbm", [D, T], F32).ap()
    gflat_d = nc.dram_tensor("gflat_d", [1, NC * H * CH], F32).ap()
    egflat_d = nc.dram_tensor("egflat_d", [1, NC * H * CH], BF16).ap()
    dbg_yn = nc.dram_tensor("dbg_yn", [128, H * T], BF16).ap()
    dbg_dc = nc.dram_tensor("dbg_dc", [128, H * HD], BF16).ap()
    dbg_ap = nc.dram_tensor("dbg_ap", [128, H * HD], BF16).ap()
    dbg_yi = nc.dram_tensor("dbg_yi", [128, H * HD], BF16).ap()
    gl_dram = nc.dram_tensor("gl_dram", [H, NC], F32).ap()
    cd_dram = nc.dram_tensor("cd_dram", [H, NC], F32).ap()
    cc_in = nc.dram_tensor("cc_in", [H * HD, HD], F32)
    cc_out = nc.dram_tensor("cc_out", [2 * H * HD, HD], F32)
    wdq_in = nc.dram_tensor("wdq_in", [4 * 128, D], BF16)
    wdq_out = nc.dram_tensor("wdq_out", [D, D], BF16)

    big0 = SB("big0", [128, 32768], BF16)
    big1 = SB("big1", [128, 8192], BF16)
    big2 = SB("big2", [128, 8192], BF16)
    big3 = SB("big3", [128, 8192], BF16)
    big5 = SB("big5", [128, 8192], BF16)
    big6 = SB("big6", [128, 4096], BF16)
    big7 = SB("big7", [128, 4096], BF16)
    big8 = SB("big8", [128, 8192], BF16)
    wbuf = [SB(f"wbuf{i}", [128, 4096], BF16) for i in range(2)]
    ebuf = [SB(f"ebuf{i}", [128, 2048], F32) for i in range(2)]
    xtile = [SB(f"xtile{i}", [128, T], F32) for i in range(2)]
    ET = SB("ET", [128, NC * H], F32)
    fv = SB("fv", [128, NC * H], F32)
    cdb = SB("cdb", [128, NC * H], F32)
    identb = SB("identb", [128, 128], BF16)

    identf = SB("identf", [128, 32], F32)
    slT = SB("slT", [128, 128], BF16)
    negidS = SB("negidS", [128, 128], BF16)
    onesb = SB("onesb", [128, 128], BF16)
    smallc = SB("smallc", [128, 16], F32)
    rT = smallc[:, 8:16]

    xc1 = big0.ap()
    ynT = big0.ap()
    kT = big1.ap()
    rbr = big1.ap().bitcast(F32)[:, 0:1024]
    vnat = big2.ap()
    wbuf2 = [big3.ap()[:, 0:4096], big3.ap()[:, 4096:8192]]
    kdec = big3.ap()[:, 0:4096]
    decayT = big3.ap()[:, 0:4096]
    apT = big3.ap()[:, 4096:8192]
    tabf = big5.ap().bitcast(F32)
    cosT = tabf[:, 0:512]
    sinT = tabf[:, 512:1024]
    cosD = big5.ap()[:, 2048:3072]
    sinD = big5.ap()[:, 3072:4096]
    resetS = tabf[0:H, 2048:3072]
    state = big5.ap().bitcast(F32)
    state_bf = big6.ap()
    egrow = big7.ap()[0:1, 2048:4096]
    glhs = big7.ap().bitcast(F32)[0:33, 0:1024]
    onesr = glhs[0:1, 0:128]
    dtp = big8.ap().bitcast(F32)
    dtnat = dtp[0:H, 0:1024]
    gnat = dtp[0:H, 1024:2048]
    Gnat = dtp[0:H, 2048:3072]
    egnat = dtp[0:H, 3072:4096]
    grhs = big8.ap().bitcast(F32)[0:33, 0:4096]
    sgbuf = big8.ap()[:, 0:8192]
    f_all = big1.ap().bitcast(F32)[:, 1024:1324]
    qbuf = wbuf[0].ap()
    csbuf = wbuf[1].ap()

    pA = st.enter_context(nc.psum_tensor("pA", [128, 2048], F32))
    pB = st.enter_context(nc.psum_tensor("pB", [128, 2048], F32))
    pAb = pA.ap().bitcast(BF16)
    pBb = pB.ap().bitcast(BF16)

    nc.const_aps.register(F32, LN_EPS, smallc[:, 5:6])
    nc.const_aps.register(F32, 1.0, smallc[:, 6:7])
    nc.const_aps.register(F32, -PI, smallc[:, 7:8])
    sems = {n: st.enter_context(nc.semaphore(n)) for n in ALL_SEMS}

    def finish():
        with st, nc.Block() as block:
            S.replay(block, sems)

    xTt = xT.rearrange("(n p) t -> n p t", p=128)
    x2t = x2_hbm.rearrange("(n p) t -> n p t", p=128)
    outr = out.rearrange("(n p) t -> n p t", p=128)

    def colslab(w, ot):
        return w[ot]

    def slab3(buf):
        return buf

    def xc1s(i, th=None):
        base = xc1[:, i * T:(i + 1) * T]
        if th is None:
            return base
        return base[:, th * 512:(th + 1) * 512]

    def ynTv(h, lo, hi):
        return ynT.rearrange("p (h t) -> p h t", h=H)[:, h, lo:hi]

    c_eye = S.dma("sync", identb[:], eye_bf)
    c_eyef = S.dma("sync", identf[:], eye_f32)
    c_sl = S.dma("sync", slT[:], sl_tri)
    c_negid = S.dma("sync", negidS[:], negid_d)
    c_ones = S.dma("sync", onesb[:], onesbf)
    gh1 = S.op("gpsimd", lambda e: e.memset(glhs[0:1, :], 1.0), inc=True)
    c_onesr = gh1
    c_cv = S.dma("sync", smallc[:, 5:8], constv)
    c_dtb = S.dma("sync", smallc[0:H, 0:1], dtb)
    c_alog = S.dma("sync", smallc[0:H, 1:2], alog)
    c_gnw = S.dma("sync", smallc[:, 2:3], gnw)
    c_ccol = S.dma("sync", smallc[:, 3:4], ccol_d)
    cst_all = ("cst", S.counts["cst"])
    for nm in ("c_eye", "c_eyef", "c_sl", "c_negid", "c_ones", "c_dtb", "c_alog", "c_gnw",
               "c_ccol", "c_cv", "c_ivf"):
        pass
    c_eye = c_eyef = c_sl = c_negid = c_ones = c_dtb = c_alog = c_gnw = c_ccol = c_cv = cst_all
    t_expA = S.op("scalar", lambda e: e.activation(out=smallc[0:H, 4:5], in_=smallc[0:H, 1:2], func=AF.Exp),
                  deps=[c_alog], inc=True)

    xt_cons = [None, None]
    eb_cons = [None, None]
    mm = None
    for i in range(ND):
        b = i % 2
        xl = S.dma("sync", xtile[b][:], xTt[i], deps=(xt_cons[b] or []), stream="xt%d" % b)
        ebb = ebuf[b][:].bitcast(BF16)
        tv = S.op("vector", lambda e, b=b, ebb=ebb: e.tensor_copy(out=ebb[:, 0:T], in_=xtile[b][:]),
                  deps=[xl, eb_cons[b]], inc=True)
        ts_ = S.op("scalar", lambda e, b=b, ebb=ebb: e.activation(out=ebb[:, T:2 * T], in_=xtile[b][:],
                                                                  func=AF.Square),
                   deps=[xl, eb_cons[b]], inc=True)
        xt_cons[b] = [tv, ts_]
        for th in range(2):
            S.op("tensor", lambda e, ebb=ebb, th=th, i=i: e.matmul(
                pA[0:1, th * 512:(th + 1) * 512], onesb[:, 0:1], ebb[:, th * 512:(th + 1) * 512],
                start=(i == 0), stop=(i == ND - 1)), deps=[tv, c_ones])
            mm = S.op("tensor", lambda e, ebb=ebb, th=th, i=i: e.matmul(
                pA[0:1, 1024 + th * 512:1024 + (th + 1) * 512], onesb[:, 0:1],
                ebb[:, 1024 + th * 512:1024 + (th + 1) * 512],
                start=(i == 0), stop=(i == ND - 1)), deps=[ts_], inc=(i == ND - 1 and th == 1))
        eb_cons[b] = S.mark("tensor")
    stats_done = mm

    murow = ebuf[1][0:1, 0:1024]
    rrow = ebuf[1][0:1, 1024:2048]
    v1row = ebuf[0][0:1, 0:1024]
    v2row = ebuf[0][0:1, 1024:2048]
    t1 = S.op("vector", lambda e: e.tensor_scalar(out=murow, in0=pA[0:1, 0:1024], scalar1=1.0 / D,
                                                  scalar2=None, op0=ALU.mult), deps=[stats_done], inc=True)
    t2 = S.op("vector", lambda e: e.tensor_scalar(out=v1row, in0=pA[0:1, 1024:2048], scalar1=1.0 / D,
                                                  scalar2=None, op0=ALU.mult), deps=[stats_done], inc=True)
    t3 = S.op("vector", lambda e: e.scalar_tensor_tensor(out=v2row, in0=murow, scalar=-1.0, in1=murow,
                                                         op0=ALU.mult, op1=ALU.mult), deps=[t1], inc=True)
    t4 = S.op("vector", lambda e: e.tensor_tensor(out=v1row, in0=v1row, in1=v2row, op=ALU.add),
              deps=[t2, t3], inc=True)
    t5 = S.op("scalar", lambda e: e.activation(out=v2row, in_=v1row, func=AF.Ln, bias=LN_EPS),
              deps=[t4, c_cv], inc=True)
    t6 = S.op("scalar", lambda e: e.activation(out=rrow, in_=v2row, func=AF.Exp, scale=-0.5),
              deps=[t5], inc=True)
    mo = None
    for th in range(2):
        mo = S.op("tensor", lambda e, th=th: e.matmul(pB[:, th * 512:(th + 1) * 512], onesr,
                                                      murow[:, th * 512:(th + 1) * 512], start=True, stop=True),
                  deps=[t1, c_onesr], inc=(th == 1))
    mub_done = mo
    ro = None
    for th in range(2):
        ro = S.op("tensor", lambda e, th=th: e.matmul(pB[:, 1024 + th * 512:1024 + (th + 1) * 512],
                                                      onesr, rrow[:, th * 512:(th + 1) * 512],
                                                      start=True, stop=True), deps=[t6], inc=(th == 1))
    rb_done = ro

    c_cos = S.dma("sync", cosD, cosd_d, deps=[], stream="cp")
    c_sin = S.dma("sync", sinD, sind_d, deps=[], stream="cp")
    xc1_list = []
    for i in range(ND):
        b = i % 2
        xl = S.dma("sync", xtile[b][:], xTt[i], deps=(xt_cons[b] or []), stream="xt%d" % b)
        tvx = S.op("vector", lambda e, b=b: e.tensor_tensor(out=xtile[b][:], in0=xtile[b][:],
                                                            in1=pB[:, 0:1024], op=ALU.subtract),
                   deps=[xl, mub_done], inc=True)
        xc1_done = S.op("vector", lambda e, b=b, i=i: e.tensor_tensor(out=xc1s(i), in0=xtile[b][:],
                                                                      in1=pB[:, 1024:2048], op=ALU.mult),
                        deps=[tvx, rb_done], inc=True)
        xc1_list.append(xc1_done)
        xt_cons[b] = [xc1_done]
    cp_all = ("cp", S.counts["cp"])
    td5 = td6 = cp_all
    if STOP_AFTER == "p0":
        S.barrier(); finish(); return nc

    wl = S.dma("gpsimd", wbuf[0][:, 0:ND * H], wint, stream="wb0")
    mm = None
    for i in range(ND):
        for th in range(2):
            mm = S.op("tensor", lambda e, i=i, th=th: e.matmul(
                pA[0:H, 1024 + th * 512:1024 + (th + 1) * 512], wbuf[0][:, i * H:(i + 1) * H],
                xc1s(i, th), start=(i == 0), stop=(i == ND - 1)),
                deps=([wl, t2] if i == 0 else []) + ([xc1_list[i]] if th == 0 else []),
                inc=(i == ND - 1 and th == 1))
    dt_done = mm
    u2 = S.op("scalar", lambda e: e.activation(out=dtnat, in_=pA[0:H, 1024:2048], func=AF.Relu,
                                               bias=smallc[0:H, 0:1]), deps=[dt_done, c_dtb], inc=True)
    escr2 = ebuf[0][0:H, 1024:2048]
    u3 = S.op("scalar", lambda e: e.activation(out=escr2, in_=pA[0:H, 1024:2048], func=AF.Abs,
                                               bias=smallc[0:H, 0:1]), deps=[dt_done], inc=True)
    u1 = u3
    u4 = S.op("scalar", lambda e: e.activation(out=escr2, in_=escr2, func=AF.Exp, scale=-1.0), deps=[u3], inc=True)
    u5 = S.op("scalar", lambda e: e.activation(out=escr2, in_=escr2, func=AF.Ln, bias=1.0), deps=[u4], inc=True)
    u6 = S.op("vector", lambda e: e.tensor_tensor(out=dtnat, in0=dtnat, in1=escr2, op=ALU.add),
              deps=[u2, u5], inc=True)
    if STOP_AFTER == "dt1":
        S.barrier(); finish(); return nc
    u7 = S.op("vector", lambda e: e.tensor_scalar(out=gnat, in0=dtnat, scalar1=smallc[0:H, 4:5],
                                                  scalar2=-1.0, op0=ALU.mult, op1=ALU.mult),
              deps=[u6, t_expA], inc=True)
    c_reset = S.dma("sync", resetS, resetm, deps=[], stream="cr")
    u8 = S.op("vector", lambda e: e.tensor_tensor_scan(out=Gnat, data0=resetS, data1=gnat,
                                                       initial=0.0, op0=ALU.mult, op1=ALU.add),
              deps=[u7, c_reset], inc=True)
    u9 = S.op("scalar", lambda e: e.activation(out=egnat, in_=Gnat, func=AF.Exp), deps=[u8], inc=True)
    if STOP_AFTER == "dt2":
        S.barrier(); finish(); return nc
    for n in range(NC):
        tt = S.op("tensor", lambda e, n=n: e.transpose(pA[:, n * 32:(n + 1) * 32],
                                                       dtnat[:, n * 128:(n + 1) * 128], identf[0:32, 0:32]),
                  deps=[u6], inc=(n == NC - 1))
    dtT_tok = tt
    for n in range(NC):
        tt = S.op("tensor", lambda e, n=n: e.transpose(pA[:, 256 + n * 32:256 + (n + 1) * 32],
                                                       Gnat[:, n * 128:(n + 1) * 128], identf[0:32, 0:32]),
                  deps=[u8], inc=(n == NC - 1))
    GT_tok = tt
    e1 = S.op("vector", lambda e: e.tensor_copy(out=fv[:], in_=pA[:, 0:256]), deps=[dtT_tok, GT_tok], inc=True)
    e2 = S.op("vector", lambda e: e.tensor_copy(out=ebuf[1][:, 512:768], in_=pA[:, 256:512]), deps=[GT_tok], inc=True)
    e3 = e1
    if STOP_AFTER == "dt3":
        S.barrier(); finish(); return nc
    gf = S.dma("gpsimd", gflat_d.rearrange("o (n h i) -> (o h) n i", n=NC, h=H),
               Gnat.rearrange("h (n i) -> h n i", n=NC), deps=[u8], stream="gm")
    egf = S.dma("gpsimd", egflat_d.rearrange("o (n h i) -> (o h) n i", n=NC, h=H),
                egnat.rearrange("h (n i) -> h n i", n=NC), deps=[u9], stream="gm")
    if STOP_AFTER == "dt3a":
        S.barrier(); finish(); return nc
    glrow = ebuf[1][0:1, 0:256]
    cdrow = ebuf[1][0:1, 256:512]
    gle = S.op("vector", lambda e: e.tensor_copy(out=ebuf[1][0:H, 768:776], in_=Gnat[:, 127::128]),
               deps=[u8], inc=True)
    cde = S.op("vector", lambda e: e.tensor_copy(out=ebuf[1][0:H, 776:784], in_=egnat[:, 127::128]),
               deps=[u9], inc=True)
    glf0 = S.dma("gpsimd", gl_dram, ebuf[1][0:H, 768:776], deps=[gle, u1], stream="gm")
    cdf0 = S.dma("gpsimd", cd_dram, ebuf[1][0:H, 776:784], deps=[cde], stream="gm")
    glf = S.dma("gpsimd", glrow.rearrange("o (n h) -> o n h", n=NC),
                gl_dram.rearrange("h n -> n h").unsqueeze(0), deps=[glf0], slow=True, stream="gm")
    cdf = S.dma("gpsimd", cdrow.rearrange("o (n h) -> o n h", n=NC),
                cd_dram.rearrange("h n -> n h").unsqueeze(0), deps=[cdf0], slow=True, stream="gm")
    if STOP_AFTER == "dt3b":
        S.barrier(); finish(); return nc
    gl1 = S.op("tensor", lambda e: e.matmul(pA[:, 512:768], onesr, glrow, start=True, stop=True),
               deps=[glf, e2], inc=True)
    cd1 = S.op("tensor", lambda e: e.matmul(pA[:, 768:1024], onesr, cdrow, start=True, stop=True),
               deps=[cdf], inc=True)
    if STOP_AFTER == "dt3c":
        S.barrier(); finish(); return nc
    e4 = S.op("vector", lambda e: e.tensor_tensor(out=ET[:], in0=pA[:, 512:768], in1=ebuf[1][:, 512:768],
                                                  op=ALU.subtract), deps=[gl1, cd1, e2], inc=True)
    e5 = S.op("scalar", lambda e: e.activation(out=ET[:], in_=ET[:], func=AF.Exp), deps=[e4], inc=True)
    if STOP_AFTER == "dt3d":
        S.barrier(); finish(); return nc
    e5b = S.op("vector", lambda e: e.tensor_tensor(out=ET[:], in0=ET[:], in1=fv[:], op=ALU.mult),
               deps=[e5, e3], inc=True)
    e6 = S.op("vector", lambda e: e.tensor_copy(out=cdb[:], in_=pA[:, 768:1024]), deps=[cd1], inc=True)
    if STOP_AFTER == "dt4":
        S.barrier(); finish(); return nc
    gh2 = S.dma("gpsimd", glhs[1:33, :], Gnat, deps=[u8], stream="gm")
    gm_all = ("gm", S.counts["gm"])
    gf = egf = glf = cdf = gh2 = gm_all
    if STOP_AFTER == "p1a":
        S.barrier(); finish(); return nc

    slab_cons = [dt_done, None]

    proj_hist = {id(pA): [e6, [u3, e6]],
                 id(pB): [xc1_list[-1], xc1_list[-1]]}
    rope_last = [None]

    def project_T(w, n_otiles, put, use_pB=False, wb=None):
        ps = pB if use_pB else pA
        res = []
        ev_hist = proj_hist[id(ps)]
        for ot in range(n_otiles):
            b = ot % 2
            half = ot % 2
            buf = (wb or wbuf)[b]
            wld = S.dma("gpsimd", buf[:, 0:4096], colslab(w, ot), deps=[slab_cons[b]],
                        stream=("wb%d" % b) if (wb is None) else ("wc%d" % b))
            mm = None
            for i in range(ND):
                for th in range(2):
                    dps = [wld] + (ev_hist[half] if isinstance(ev_hist[half], list)
                                   else [ev_hist[half]]) if i == 0 else []
                    if th == 0:
                        dps = dps + [xc1_list[i]]
                    mm = S.op("tensor", lambda e, buf=buf, i=i, th=th, ps=ps, half=half: e.matmul(
                        ps[:, half * 1024 + th * 512:half * 1024 + (th + 1) * 512],
                        buf[:, i * 128:(i + 1) * 128], xc1s(i, th),
                        start=(i == 0), stop=(i == ND - 1)),
                        deps=dps, inc=(i == ND - 1 and th == 1))
            slab_cons[b] = mm
            tok = put(ot, mm, ps[:, half * 1024:(half + 1) * 1024])
            res.append(tok)
            ev_hist[half] = tok
        return res

    def rope_T(dst, dep0, scratch_bf):
        lo, hi = dst[0:64, :], dst[64:128, :]
        w0 = scratch_bf
        rot, t1, t2 = w0[:, 0:1024], w0[:, 1024:2048], w0[:, 2048:3072]
        a1 = S.op("vector", lambda e: e.tensor_scalar(out=rot[0:64, :], in0=hi, scalar1=-1.0,
                                                      scalar2=None, op0=ALU.mult),
                  deps=[dep0, td5, td6], inc=True)
        a2 = S.op("vector", lambda e: e.tensor_scalar(out=rot[64:128, :], in0=lo, scalar1=1.0,
                                                      scalar2=None, op0=ALU.mult), inc=True)
        a3 = S.op("vector", lambda e: e.tensor_tensor(out=t1, in0=dst, in1=cosD, op=ALU.mult),
                  deps=[a1], inc=True)
        a4 = S.op("vector", lambda e: e.tensor_tensor(out=t2, in0=rot, in1=sinD, op=ALU.mult),
                  deps=[a2], inc=True)
        a5 = S.op("vector", lambda e: e.tensor_tensor(out=dst, in0=t1, in1=t2, op=ALU.add),
                  deps=[a3, a4], inc=True)
        return a5, a5

    def k_put(ot, mm, psv):
        tgt = kT[:, ot * T:(ot + 1) * T]
        ev = S.op("scalar", lambda e, tgt=tgt, psv=psv: e.activation(out=tgt, in_=psv, func=AF.Copy),
                  deps=[mm], inc=True)
        rope_last[0] = rope_T(tgt, ev, ebuf[0][:].bitcast(BF16))[0]
        return ev

    def v_put(ot, mm, psv):
        tgt = vnat
        evb = ebuf[0][:].bitcast(BF16)[:, 0:1024]
        ev = S.op("scalar", lambda e, evb=evb, psv=psv: e.activation(out=evb, in_=psv, func=AF.Copy),
                  deps=[mm, rope_last[0]], inc=True)
        for n in range(NC):
            ttk = S.op("tensor", lambda e, n=n, evb=evb: e.transpose(
                pBb[:, (n % 8) * 128:((n % 8) + 1) * 128],
                evb[:, n * 128:(n + 1) * 128], identb[:]),
                deps=([ev, xc1_list[-1]] if n == 0 else []), inc=(n == NC - 1))
        evs = S.op("vector", lambda e, ot=ot: e.tensor_copy(
            out=vnat.rearrange("p (n kv d) -> p n kv d", n=NC, kv=KV)[:, :, ot, :],
            in_=pBb[:, 0:1024].rearrange("p (n d) -> p n d", n=NC)), deps=[ttk], inc=True)
        S.wait("tensor", evs)
        S.wait("scalar", evs)
        return ev

    k_res = project_T(wkt, KV, k_put)
    v_res = project_T(wvt, KV, v_put)
    proj_hist[id(pB)] = [[v_res[-1], xc1_list[-1]], [xc1_list[-1]]]
    if STOP_AFTER == "p1b":
        S.barrier(); finish(); return nc

    def q_put(ot, mm, psv):
        dst = ebuf[0][:].bitcast(BF16)[:, (ot % 2) * 1024:(ot % 2) * 1024 + 1024]
        ev = S.op("scalar", lambda e, dst=dst, psv=psv: e.activation(out=dst, in_=psv, func=AF.Copy),
                  deps=[mm] + ([qsp_hist[ot % 2]] if qsp_hist[ot % 2] else []), inc=True)
        a5, a6 = rope_T(dst, ev, ebuf[1][:].bitcast(BF16))
        rope_last[0] = a5
        sp = S.dma("scalar", q_spill.rearrange("(a p) t -> a p t", p=128)[ot], dst, deps=[a5],
                   stream="qo")
        qsp_hist[ot % 2] = sp
        return ev

    qsp_hist = [None, None]
    q_res = project_T(wqt, ND, q_put, use_pB=True, wb=wbuf2)
    q_sp_last = ("qo", S.counts["qo"])

    def g_put(ot, mm, psv):
        slb = ebuf[1][:].bitcast(BF16)[:, (ot % 2) * 1024:(ot % 2) * 1024 + 1024]
        sl = S.op("scalar", lambda e, slb=slb, psv=psv: e.activation(out=slb, in_=psv, func=AF.Silu),
                  deps=[mm, rope_last[0]], inc=True)
        gout = ebuf[0][:].bitcast(BF16)[:, (ot % 2) * 1024:(ot % 2) * 1024 + 1024]
        ev = S.op("vector", lambda e, slb=slb, gout=gout: e.tensor_scalar(
            out=gout, in0=slb, scalar1=smallc[:, 2:3], scalar2=None, op0=ALU.mult),
            deps=[sl, c_gnw] + ([gsp_hist[ot % 2]] if gsp_hist[ot % 2] else []), inc=True)
        sp = S.dma("scalar", sg_spill.rearrange("(a p) t -> a p t", p=128)[ot],
                   gout, deps=[ev], stream="go")
        gsp_hist[ot % 2] = sp
        return sl

    gsp_hist = list(qsp_hist)
    g_res = project_T(wgt, ND, g_put, use_pB=True, wb=wbuf2)
    g_sp_last = ("go", S.counts["go"])
    S.barrier()
    if STOP_AFTER == "p1c":
        finish()
        return nc

    zst = S.op("gpsimd", lambda e: e.memset(state[:], 0.0), inc=True)
    wot_pc = []
    for ot in range(ND):
        wot_pc.append(S.dma("gpsimd", wot_bf[ot], wot[ot], stream="wb%d" % (ot % 2)))
    cs_sp = []
    scan_tok = zst
    ce_prev = None
    for n in range(NC):
        for kv in range(KV):
            ttk = S.op("tensor", lambda e, kv=kv, n=n: e.transpose(
                pAb[:, 1024 + kv * 128:1024 + (kv + 1) * 128],
                kT[:, kv * T + n * 128:kv * T + (n + 1) * 128], identb[:]),
                deps=([scan_tok, e5b, ce_prev, zst] if kv == 0 else []), inc=(kv == KV - 1))
        kd = S.op("vector", lambda e, n=n: e.tensor_tensor(
            out=kdec.rearrange("p (kv r d) -> p kv r d", kv=KV, r=4),
            in0=pAb[:, 1024:2048].rearrange("p (kv d) -> p kv d", kv=KV)
                .unsqueeze(2).broadcast_to([128, KV, 4, HD]),
            in1=ET[:, n * H:(n + 1) * H].rearrange("p (kv r) -> p kv r", kv=KV)
                .unsqueeze(3).broadcast_to([128, KV, 4, HD]),
            op=ALU.mult), deps=[ttk], inc=True)
        S.wait("tensor", kd)
        cs_toks = []
        for h in range(H):
            ps = pA if h < 16 else pB
            off = (h % 16) * 128
            tm = S.op("tensor", lambda e, h=h, ps=ps, off=off, n=n: e.matmul(
                ps[:, off:off + 128], kdec[:, h * 128:(h + 1) * 128],
                vnat[:, n * 1024 + (h // 4) * 128:n * 1024 + (h // 4 + 1) * 128],
                start=True, stop=True), deps=([kd] if h == 0 else []), inc=(h in (15, 31)))
            if h in (15, 31):
                cs_toks.append(tm)
        ce1 = S.op("scalar", lambda e: e.activation(out=csbuf[:, 0:2048], in_=pA[:], func=AF.Copy),
                   deps=[cs_toks[0]] + ([cs_sp[-1]] if cs_sp else []), inc=True)
        ce2 = S.op("scalar", lambda e: e.activation(out=csbuf[:, 2048:4096], in_=pB[:], func=AF.Copy),
                   deps=[cs_toks[1]], inc=True)
        ce_prev = ce2
        csp = S.dma("scalar", cs_spill[n], csbuf[:], deps=[ce1, ce2], stream="co")
        cs_sp.append(csp)
        s1 = S.op("vector", lambda e, n=n: e.tensor_tensor(
            out=state[:].rearrange("p (h d) -> p h d", h=H),
            in0=state[:].rearrange("p (h d) -> p h d", h=H),
            in1=cdb[:, n * H:(n + 1) * H].unsqueeze(2).broadcast_to([128, H, HD]),
            op=ALU.mult), deps=[e6, kd], inc=True)
        s2 = S.op("vector", lambda e: e.tensor_tensor(out=state[:, 0:2048], in0=state[:, 0:2048],
                                                      in1=pA[:], op=ALU.add), deps=[s1, cs_toks[0]], inc=True)
        s3 = S.op("vector", lambda e: e.tensor_tensor(out=state[:, 2048:4096], in0=state[:, 2048:4096],
                                                      in1=pB[:], op=ALU.add), deps=[s1, cs_toks[1]], inc=True)
        scan_tok = s3
        S.wait("tensor", s2)
        S.wait("tensor", s3)
    c_negb = S.dma("sync", grhs[1:33, :], negblock, stream="cn")
    qsp = q_spill.rearrange("(a p) t -> p a t", p=128)

    def load_q(n, deps):
        toks = [None, None]
        for gidx in range(8):
            toks[gidx % 2] = S.dma(
                "sync", qbuf[:, gidx * 512:(gidx + 1) * 512].rearrange("p (a t) -> p a t", a=4),
                qsp[:, gidx * 4:(gidx + 1) * 4, n * 128:(n + 1) * 128],
                deps=deps if gidx < 2 else [], stream="qb" if gidx % 2 == 0 else "qb2")
        return toks

    qld = load_q(0, [q_sp_last, slab_cons[0], slab_cons[1]])
    xtb = [xtile[0].ap().bitcast(BF16), xtile[1].ap().bitcast(BF16)]

    def cs_bufs(n):
        if n % 2 == 0:
            return xtb[0][:], xtb[1][:], "cl", "cl2"
        return csbuf[:, 0:2048], csbuf[:, 2048:4096], "cl3", "cl4"

    def cs_load(n, deps_a, deps_b):
        ba, bb_, sa, sb_ = cs_bufs(n)
        ta = S.dma("sync", ba, cs_spill[n][:, 0:2048], deps=deps_a, stream=sa)
        tb_ = S.dma("sync", bb_, cs_spill[n][:, 2048:4096], deps=deps_b, stream=sb_)
        return ta, tb_

    cs_ld = {}
    cs_ld[0] = cs_load(0, [xc1_list[30], cs_sp[0]], [xc1_list[31], cs_sp[0]])
    cs_ld[1] = cs_load(1, [cs_sp[-1]], [cs_sp[-1]])
    grl = S.dma("sync", grhs[0:1, :], gflat_d[:, 0:4096], deps=[gf, c_negb], stream="gr")
    egl0 = S.dma("sync", egrow, egflat_d[:, 0:4096 // 2], deps=[egf], stream="eg")
    st_out = S.dma("gpsimd", cc_in.ap().rearrange("(h d) e -> d h e", h=H),
                   state[:].rearrange("d (h e) -> d h e", h=H), deps=[scan_tok], stream="gm")
    S.waits("gpsimd", [st_out])
    S.counts["sC"] += 1
    cc_tok = ("sC", S.counts["sC"])

    def cc_fn(e):
        return e.collective_compute("AllGather", ALU.bypass, replica_groups=replica_groups,
                                    ins=[cc_in.ap().opt()], outs=[cc_out.ap().opt()])
    S.items["gpsimd"].append(["op", cc_fn, "sC"])
    tld = S.dma("gpsimd", state[:].rearrange("d (h e) -> d h e", h=H),
                cc_out.ap()[0:H * HD, :].rearrange("(h d) e -> d h e", h=H), deps=[cc_tok], stream="gm")
    tc0 = S.op("vector", lambda e: e.tensor_scalar(out=state[:], in0=state[:], scalar1=smallc[:, 3:4],
                                                   scalar2=None, op0=ALU.mult), deps=[tld, c_ccol], inc=True)
    tb0 = S.op("scalar", lambda e: e.activation(out=state_bf[:], in_=state[:], func=AF.Copy),
               deps=[tc0], inc=True)
    if STOP_AFTER == "p2":
        S.barrier(); finish(); return nc

    st_up_prev = tc0
    sbf_prev = tb0
    pa_free = None
    pb_free = None
    egl_prev = None
    for n in range(NC):
        qld_n, grl_n = qld, grl
        cs_a, cs_b = cs_ld[n]
        buf_a, buf_b = cs_bufs(n)[0:2]
        ymm_b = None
        dmm = None
        for hf in range(2):
            if n == 0 and hf == 0:
                egl = egl0
            else:
                egl = S.dma("sync", egrow, egflat_d[:, n * 4096 + hf * 2048:n * 4096 + (hf + 1) * 2048],
                            deps=[egf, egl_prev], stream="eg")
            dmm = None
            for sl in range(4):
                dmm = S.op("tensor", lambda e, sl=sl, hf=hf, n=n: e.matmul(
                    pA[:, sl * 512:(sl + 1) * 512], glhs[:, n * 128:(n + 1) * 128],
                    grhs[:, hf * 2048 + sl * 512:hf * 2048 + (sl + 1) * 512],
                    start=True, stop=False),
                    deps=([grl_n, gh1, gh2, pa_free, scan_tok, ce_prev] if sl == 0 else []))
            for sl in range(4):
                dmm = S.op("tensor", lambda e, sl=sl: e.matmul(
                    pA[:, sl * 512:(sl + 1) * 512], slT[:],
                    negidS[:].unsqueeze(1).broadcast_to([128, 4, 128]),
                    start=False, stop=True), deps=([c_sl, c_negid] if sl == 0 else []), inc=(sl == 3))
            dc = S.op("scalar", lambda e, hf=hf: e.activation(out=decayT[:, hf * 2048:(hf + 1) * 2048],
                                                              in_=pA[:], func=AF.Exp), deps=[dmm], inc=True)
            if DEBUG_SPILL and n == 1:
                S.dma("scalar", dbg_dc[:, hf * 2048:(hf + 1) * 2048], decayT[:, hf * 2048:(hf + 1) * 2048],
                      deps=[dc], stream="ro")
            smm = None
            for h in range(16):
                hh = hf * 16 + h
                smm = S.op("tensor", lambda e, hh=hh, h=h, n=n: e.matmul(
                    pA[:, h * 128:(h + 1) * 128], kT[:, (hh // 4) * T + n * 128:(hh // 4) * T + (n + 1) * 128],
                    qbuf[:, hh * 128:(hh + 1) * 128], start=True, stop=True),
                    deps=([dc] + qld_n if h == 0 else []), inc=(h == 15))
            ap_ = S.op("vector", lambda e, hf=hf: e.tensor_tensor(
                out=apT[:, hf * 2048:(hf + 1) * 2048], in0=pA[:], in1=decayT[:, hf * 2048:(hf + 1) * 2048],
                op=ALU.mult), deps=[smm, dc], inc=True)
            ap2 = S.op("vector", lambda e, hf=hf, n=n: e.tensor_tensor(
                out=apT[:, hf * 2048:(hf + 1) * 2048].rearrange("p (h i) -> p h i", h=16),
                in0=apT[:, hf * 2048:(hf + 1) * 2048].rearrange("p (h i) -> p h i", h=16),
                in1=fv[:, n * H + hf * 16:n * H + (hf + 1) * 16].unsqueeze(2).broadcast_to([128, 16, 128]),
                op=ALU.mult), deps=[ap_, e3], inc=True)
            if DEBUG_SPILL and n == 1:
                S.dma("scalar", dbg_ap[:, hf * 2048:(hf + 1) * 2048], apT[:, hf * 2048:(hf + 1) * 2048],
                      deps=[ap2], stream="ro")
            ymm_a = None
            for h in range(16):
                hh = hf * 16 + h
                ymm_a = S.op("tensor", lambda e, hh=hh, h=h, n=n: e.matmul(
                    pA[:, h * 128:(h + 1) * 128],
                    vnat[:, n * 1024 + (hh // 4) * 128:n * 1024 + (hh // 4 + 1) * 128],
                    apT[:, hh * 128:(hh + 1) * 128], start=True, stop=True),
                    deps=([ap2] if h == 0 else []), inc=(h == 15))
            for h in range(16):
                hh = hf * 16 + h
                ymm_b = S.op("tensor", lambda e, hh=hh, h=h: e.matmul(
                    pB[:, h * 128:(h + 1) * 128], state_bf[:, hh * 128:(hh + 1) * 128],
                    qbuf[:, hh * 128:(hh + 1) * 128], start=True, stop=True),
                    deps=([sbf_prev, pb_free] if h == 0 else []), inc=(h == 15))
            yi = S.op("scalar", lambda e: e.activation(out=ebuf[0][:].bitcast(BF16)[:, 0:2048], in_=pB[:],
                                                       func=AF.Copy), deps=[ymm_b], inc=True)
            if DEBUG_SPILL and n == 1:
                S.dma("scalar", dbg_yi[:, hf * 2048:(hf + 1) * 2048], ebuf[0][:].bitcast(BF16)[:, 0:2048],
                      deps=[yi], stream="ro")
            ebmm = None
            for sl in range(4):
                ebmm = S.op("tensor", lambda e, sl=sl: e.matmul(
                    pB[:, sl * 512:(sl + 1) * 512], onesb[0:1, :],
                    egrow[:, sl * 512:(sl + 1) * 512], start=True, stop=True),
                    deps=([yi, egl] if sl == 0 else []), inc=(sl == 3))
            egl_prev = ebmm
            y1 = S.op("vector", lambda e: e.tensor_tensor(out=ebuf[1][:, 0:2048],
                                                          in0=ebuf[0][:].bitcast(BF16)[:, 0:2048],
                                                          in1=pB[:], op=ALU.mult), deps=[ebmm, yi], inc=True)
            y2 = S.op("vector", lambda e: e.tensor_tensor(out=ebuf[1][:, 0:2048], in0=ebuf[1][:, 0:2048],
                                                          in1=pA[:], op=ALU.add), deps=[y1, ymm_a], inc=True)
            pa_free = y2
            sq = S.op("vector", lambda e: e.tensor_tensor(out=ebuf[0][:].bitcast(BF16)[:, 2048:4096],
                                                          in0=ebuf[1][:, 0:2048], in1=ebuf[1][:, 0:2048],
                                                          op=ALU.mult), deps=[y2], inc=True)
            rmm = None
            for sl in range(4):
                rmm = S.op("tensor", lambda e, sl=sl: e.matmul(
                    pB[:, sl * 512:(sl + 1) * 512], onesb[:],
                    ebuf[0][:].bitcast(BF16)[:, 2048 + sl * 512:2048 + (sl + 1) * 512],
                    start=True, stop=True), deps=([sq, y1] if sl == 0 else []), inc=(sl == 3))
            r1 = S.op("scalar", lambda e: e.activation(out=ebuf[0][:, 0:2048], in_=pB[:], func=AF.Ln,
                                                       scale=1.0 / HD, bias=RMS_EPS), deps=[rmm, yi, c_cv], inc=True)
            r2 = S.op("scalar", lambda e: e.activation(out=ebuf[0][:, 0:2048], in_=ebuf[0][:, 0:2048],
                                                       func=AF.Exp, scale=-0.5), deps=[r1], inc=True)
            pb_free = r1
            yt = S.op("vector", lambda e, n=n, hf=hf: e.tensor_tensor(
                out=ynT.rearrange("p (h nn i) -> p h nn i", h=H, nn=NC)[:, hf * 16:(hf + 1) * 16, n, :],
                in0=ebuf[1][:, 0:2048].rearrange("p (h i) -> p h i", h=16),
                in1=ebuf[0][:, 0:2048].rearrange("p (h i) -> p h i", h=16), op=ALU.mult),
                deps=[y2, r2], inc=True)
        if n + 1 < NC:
            qld = load_q(n + 1, [ymm_b])
            grl = S.dma("sync", grhs[0:1, :], gflat_d[:, (n + 1) * 4096:(n + 2) * 4096], deps=[dmm],
                        stream="gr")
        su1 = S.op("vector", lambda e, n=n: e.tensor_tensor(
            out=state[:].rearrange("p (h d) -> p h d", h=H),
            in0=state[:].rearrange("p (h d) -> p h d", h=H),
            in1=cdb[:, n * H:(n + 1) * H].unsqueeze(2).broadcast_to([128, H, HD]),
            op=ALU.mult), deps=[e6, sbf_prev, st_up_prev], inc=True)
        su2a = S.op("vector", lambda e, buf_a=buf_a: e.tensor_tensor(
            out=state[:, 0:2048], in0=state[:, 0:2048], in1=buf_a, op=ALU.add),
            deps=[su1, cs_a], inc=True)
        su2 = S.op("vector", lambda e, buf_b=buf_b: e.tensor_tensor(
            out=state[:, 2048:4096], in0=state[:, 2048:4096], in1=buf_b, op=ALU.add),
            deps=[cs_b], inc=True)
        st_up_prev = su2
        if n + 2 < NC:
            cs_ld[n + 2] = cs_load(n + 2, [su2a], [su2])
        if n + 1 < NC:
            sbf = S.op("scalar", lambda e: e.activation(out=state_bf[:], in_=state[:], func=AF.Copy),
                       deps=[su2, ymm_b], inc=True)
            sbf_prev = sbf
    if DEBUG_SPILL:
        S.dma("scalar", dbg_yn, ynT, deps=[ydone_prev] if False else [yt], stream="ro")
    S.barrier()
    if STOP_AFTER == "p3":
        finish()
        return nc

    sgsp = sg_spill.rearrange("(a p) t -> p a t", p=128)
    ygt = None
    gate_th = [None, None]
    sg_cons = [None, None, None, None]
    for idx in range(16):
        th, hg = idx // 8, idx % 8
        slot = idx % 4
        sgl = S.dma("sync", sgbuf[:, slot * 2048:(slot + 1) * 2048].rearrange("p (a t) -> p a t", a=4),
                    sgsp[:, hg * 4:(hg + 1) * 4, th * 512:(th + 1) * 512],
                    deps=[g_sp_last, sg_cons[slot]], stream="sg%d" % slot)
        ygt = S.op("vector", lambda e, th=th, hg=hg, slot=slot: e.tensor_tensor(
            out=ynT.rearrange("p (h t) -> p h t", h=H)[:, hg * 4:(hg + 1) * 4, th * 512:(th + 1) * 512],
            in0=ynT.rearrange("p (h t) -> p h t", h=H)[:, hg * 4:(hg + 1) * 4, th * 512:(th + 1) * 512],
            in1=sgbuf[:, slot * 2048:(slot + 1) * 2048].rearrange("p (a t) -> p a t", a=4),
            op=ALU.mult), deps=[sgl], inc=True)
        sg_cons[slot] = ygt
        gate_th[th] = ygt
    gate_all = ygt

    c_fs = S.dma("sync", smallc[:, 5:9] if False else f_all[:, 0:128], wq_fs, stream="c5")
    c_zs = S.dma("sync", f_all[:, 128:256], wq_zs, stream="c5")
    c_s2s = S.dma("sync", f_all[:, 256:260], wq_s2s, stream="c5")
    c5b = ("c5", S.counts["c5"])
    ff1 = S.op("vector", lambda e: e.tensor_tensor(
        out=f_all[:, 0:128].rearrange("p (j g) -> p j g", j=4),
        in0=f_all[:, 0:128].rearrange("p (j g) -> p j g", j=4),
        in1=f_all[:, 256:260].unsqueeze(2).broadcast_to([128, 4, 32]),
        op=ALU.mult), deps=[c5b], inc=True)
    qslab = big2.ap()[:, 0:4096]
    mslab = big2.ap()[:, 4096:8192]
    dqs = big3.ap().bitcast(F32)
    wts = big6.ap()
    _ffn_state = {"fo": None, "dq": None}

    def emit_ffn_dequant(j):
        fql = S.dma("gpsimd", qslab, wq_qs[j], deps=[_ffn_state["dq"]], stream="fq")
        fml = S.dma("gpsimd", mslab, wq_ms[j], deps=[_ffn_state["dq"]], stream="fm")
        fd1 = S.op("vector", lambda e, j=j: e.tensor_tensor(
            out=dqs.rearrange("p (g i) -> p g i", g=32),
            in0=qslab.rearrange("p (g i) -> p g i", g=32),
            in1=f_all[:, 128 + j * 32:128 + (j + 1) * 32].unsqueeze(2).broadcast_to([128, 32, 128]),
            op=ALU.subtract), deps=[fql, ff1], inc=True)
        fd2 = S.op("vector", lambda e, j=j: e.tensor_tensor(
            out=dqs.rearrange("p (g i) -> p g i", g=32),
            in0=dqs.rearrange("p (g i) -> p g i", g=32),
            in1=f_all[:, j * 32:(j + 1) * 32].unsqueeze(2).broadcast_to([128, 32, 128]),
            op=ALU.mult), deps=[fd1], inc=True)
        fd3 = S.op("vector", lambda e: e.tensor_tensor(out=qslab, in0=dqs, in1=mslab,
                                                       op=ALU.mult), deps=[fd2, fml], inc=True)
        tev3 = []
        for nb in range(32):
            fttk = S.op("tensor", lambda e, nb=nb: e.transpose(
                pAb[:, 2048 + (nb % 4) * 128:2048 + ((nb % 4) + 1) * 128],
                qslab[:, nb * 128:(nb + 1) * 128], identb[:]),
                deps=([fd3, _ffn_state["fo"]] if nb == 0 else []), inc=(nb % 4 == 3))
            if nb % 4 == 3:
                fev = S.op("vector", lambda e, nb=nb: e.tensor_copy(
                    out=wts[:, (nb - 3) * 128:(nb + 1) * 128], in_=pAb[:, 2048:2560]),
                    deps=[fttk], inc=True)
                tev3.append(fev)
                S.wait("tensor", fev)
        _ffn_state["dq"] = fd3
        _ffn_state["tev3"] = tev3

    def emit_ffn_store(j):
        _ffn_state["fo"] = S.dma("gpsimd", wdq_in.rearrange("(j p) n -> j p n", p=128)[j], wts[:],
                                 deps=_ffn_state["tev3"], stream="fo")


    wo_cons = [None, None]
    xt_cons = [None, None]
    xsp_cons = [None, None]
    st2 = None
    stat_cons = None
    xsp_list = []
    def cc2_fn(e):
        return e.collective_compute("AllGather", ALU.bypass, replica_groups=[list(range(8))],
                                    ins=[wdq_in.ap().opt()], outs=[wdq_out.ap().opt()])

    cc2_tok = None
    for ot in range(ND):
        bb = ot % 2
        if ot >= 7 and (ot - 7) % 4 == 0 and (ot - 7) // 4 < 4:
            emit_ffn_store((ot - 7) // 4)
        if ot == 21:
            S.waits("gpsimd", [_ffn_state["fo"]])
            S.counts["sC"] += 1
            cc2_tok = ("sC", S.counts["sC"])
            S.items["gpsimd"].append(["op", cc2_fn, "sC"])
        wld = S.dma("sync", wbuf[bb][:, 0:4096], wot_bf[ot], deps=[wo_cons[bb], tb0, wot_pc[ot]],
                    stream="wo%d" % bb)
        mm = None
        for th in range(2):
            for hd in range(ND):
                dps = []
                if hd == 0:
                    if th == 0:
                        dps.append(wld)
                    if ot == 0:
                        dps.append(gate_th[th])
                mm = S.op("tensor", lambda e, bb=bb, th=th, hd=hd: e.matmul(
                    pA[:, th * 512:(th + 1) * 512], wbuf[bb][:, hd * 128:(hd + 1) * 128],
                    ynTv(hd, th * 512, (th + 1) * 512), start=(hd == 0), stop=(hd == ND - 1)),
                    deps=dps, inc=(th == 1 and hd == ND - 1))
        wo_cons[bb] = mm
        if ot >= 4 and ot % 4 == 0 and (ot - 4) // 4 < 4:
            emit_ffn_dequant((ot - 4) // 4)
        xl = S.dma("sync", xtile[bb][:], xTt[ot], deps=[xt_cons[bb], xsp_cons[bb]], stream="xt%d" % bb)
        x2v = S.op("vector", lambda e, bb=bb: e.tensor_tensor(out=xtile[bb][:], in0=xtile[bb][:],
                                                              in1=pA[:, 0:1024], op=ALU.add),
                   deps=[xl, mm], inc=True)
        S.wait("tensor", x2v)
        xsp = S.dma("scalar", x2t[ot], xtile[bb][:], deps=[x2v], stream="x2o" if bb == 0 else "x2p")
        xsp_cons[bb] = xsp
        xsp_list.append(xsp)
        cb = S.op("vector", lambda e, bb=bb: e.tensor_copy(out=ebuf[0][:].bitcast(BF16)[:, 0:1024],
                                                           in_=xtile[bb][:]), deps=[x2v, stat_cons], inc=True)
        cs = S.op("scalar", lambda e, bb=bb: e.activation(out=ebuf[0][:].bitcast(BF16)[:, 1024:2048],
                                                          in_=xtile[bb][:], func=AF.Square),
                  deps=[x2v, stat_cons, xsp, cb], inc=True)
        xt_cons[bb] = cs
        for th in range(2):
            S.op("tensor", lambda e, th=th, ot=ot: e.matmul(
                pB[0:1, th * 512:(th + 1) * 512], onesb[:, 0:1],
                ebuf[0][:].bitcast(BF16)[:, th * 512:(th + 1) * 512],
                start=(ot == 0), stop=(ot == ND - 1)), deps=[cb])
            st2 = S.op("tensor", lambda e, th=th, ot=ot: e.matmul(
                pB[0:1, 1024 + th * 512:1024 + (th + 1) * 512], onesb[:, 0:1],
                ebuf[0][:].bitcast(BF16)[:, 1024 + th * 512:1024 + (th + 1) * 512],
                start=(ot == 0), stop=(ot == ND - 1)), deps=[cs], inc=True)
        stat_cons = st2

    mu2 = ebuf[1][0:1, 0:1024]
    r2r = ebuf[1][0:1, 1024:2048]
    w1 = ebuf[0][0:1, 0:1024]
    w2 = ebuf[0][0:1, 1024:2048]
    m1 = S.op("vector", lambda e: e.tensor_scalar(out=mu2, in0=pB[0:1, 0:1024], scalar1=1.0 / D,
                                                  scalar2=None, op0=ALU.mult), deps=[st2], inc=True)
    m2 = S.op("vector", lambda e: e.tensor_scalar(out=w1, in0=pB[0:1, 1024:2048], scalar1=1.0 / D,
                                                  scalar2=None, op0=ALU.mult), deps=[st2], inc=True)
    m3 = S.op("vector", lambda e: e.scalar_tensor_tensor(out=w2, in0=mu2, scalar=-1.0, in1=mu2,
                                                         op0=ALU.mult, op1=ALU.mult), deps=[m1], inc=True)
    m4 = S.op("vector", lambda e: e.tensor_tensor(out=w1, in0=w1, in1=w2, op=ALU.add), deps=[m2, m3], inc=True)
    m5 = S.op("scalar", lambda e: e.activation(out=w2, in_=w1, func=AF.Ln, bias=LN_EPS), deps=[m4, c_cv], inc=True)
    m6 = S.op("scalar", lambda e: e.activation(out=r2r, in_=w2, func=AF.Exp, scale=-0.5), deps=[m5], inc=True)

    mo2 = None
    for th in range(2):
        mo2 = S.op("tensor", lambda e, th=th: e.matmul(pA[:, th * 512:(th + 1) * 512], onesr,
                                                       mu2[:, th * 512:(th + 1) * 512], start=True, stop=True),
                   deps=[m1, m6, st2], inc=(th == 1))
    ro2 = None
    for th in range(2):
        ro2 = S.op("tensor", lambda e, th=th: e.matmul(pA[:, 1024 + th * 512:1024 + (th + 1) * 512],
                                                       onesr, r2r[:, th * 512:(th + 1) * 512],
                                                       start=True, stop=True), deps=[m6, m2], inc=(th == 1))

    muB = ebuf[0][:, 0:1024]
    rB = ebuf[0][:, 1024:2048]
    cpyMu = S.op("scalar", lambda e: e.activation(out=muB, in_=pA[:, 0:1024], func=AF.Copy),
                 deps=[mo2], inc=True)
    cpyR = S.op("vector", lambda e: e.tensor_copy(out=rB, in_=pA[:, 1024:2048]),
                deps=[ro2], inc=True)

    biasrow = big8.ap()[0:1, 4096:8192]
    c_bias = S.dma("gpsimd", biasrow, wq_b, deps=[gate_all], stream="cb")
    wq_cons = [None, None]
    outr2 = wdq_out.rearrange("(kt p) n -> kt p n", p=128)
    wl5_pre = [None, None]
    for kt in (0, 1):
        wl5_pre[kt] = S.dma("gpsimd", wbuf[kt][:], outr2[kt],
                            deps=[wo_cons[kt], cc2_tok], stream="w5%d" % kt)

    xc2_done = []
    for i in range(ND):
        bb = i % 2
        eng = "vector" if bb == 0 else "gpsimd"
        xl = S.dma("sync", xtile[bb][:], x2t[i], deps=[xsp_list[i], xt_cons[bb]], stream="xt%d" % bb)
        tv = S.op(eng, lambda e, bb=bb: e.tensor_tensor(out=xtile[bb][:], in0=xtile[bb][:],
                                                        in1=muB, op=ALU.subtract),
                  deps=[xl, cpyMu, gate_all], inc=True)
        tv2 = S.op(eng, lambda e, bb=bb, i=i: e.tensor_tensor(out=xc1s(i), in0=xtile[bb][:],
                                                              in1=rB, op=ALU.mult),
                   deps=[tv, cpyR], inc=True)
        xt_cons[bb] = tv2
        xc2_done.append(tv2)
    xc2_all = xc2_done[-1]
    if STOP_AFTER == "p4":
        S.barrier(); finish(); return nc

    out_toks = []
    o2_cons = [None, None]
    o1_tok = [None, None]
    for kt in range(32):
        bb = kt % 2
        po = bb * 1024
        if kt < 2:
            wl5 = wl5_pre[kt]
        else:
            wl5 = S.dma("gpsimd", wbuf[bb][:], outr2[kt], deps=[wq_cons[bb], cc2_tok],
                        stream="w5%d" % bb)
        xl = S.dma("sync", xtile[bb][:], x2t[kt],
                   deps=[o2_cons[bb]] + ([xc2_done[30 + bb]] if kt < 2 else []), stream="xt%d" % bb)
        for th in range(2):
            mm = None
            for nb in range(32):
                mm = S.op("tensor", lambda e, th=th, nb=nb, bb=bb, po=po: e.matmul(
                    pB[:, po + th * 512:po + (th + 1) * 512], wbuf[bb][:, nb * 128:(nb + 1) * 128],
                    xc1s(nb, th), start=(nb == 0), stop=False),
                    deps=([wl5, o1_tok[bb], xc2_done[nb]] if nb == 0 else [xc2_done[nb]]))
            mm = S.op("tensor", lambda e, th=th, kt=kt, po=po: e.matmul(
                pB[:, po + th * 512:po + (th + 1) * 512], biasrow[0:1, kt * 128:(kt + 1) * 128],
                onesb[0:1, :].unsqueeze(1).broadcast_to([1, 4, 128]),
                start=False, stop=True), deps=[c_bias], inc=True)
            slot = (2 * bb + th) * 512
            o2 = S.op("vector", lambda e, th=th, bb=bb, po=po, slot=slot: e.tensor_tensor(
                out=ebuf[1][:, slot:slot + 512], in0=pB[:, po + th * 512:po + (th + 1) * 512],
                in1=xtile[bb][:, th * 512:(th + 1) * 512], op=ALU.add),
                deps=[mm, xl] + ([out_toks[-4]] if len(out_toks) >= 4 else []), inc=True)
            osp = S.dma("scalar", outr[kt][:, th * 512:(th + 1) * 512], ebuf[1][:, slot:slot + 512],
                        deps=[o2], stream="oo")
            out_toks.append(osp)
            o1_tok[bb] = o2
        wq_cons[bb] = mm
        o2_cons[bb] = o2

    S.wait("scalar", ("oo", S.counts["oo"]))
    S.wait("scalar", ("x2o", S.counts["x2o"]))
    S.wait("scalar", ("x2p", S.counts["x2p"]))

    finish()
    return nc


def _resetm():
    r = np.ones((H, T), np.float32)
    r[:, ::CH] = 0.0
    return r


def make_consts():
    return dict(
        eye_bf=np.eye(128, dtype=BF),
        eye_f32=np.tile(np.eye(32, dtype=np.float32), (4, 1)),
        sl_tri=np.triu(np.ones((128, 128), np.float32), 1).astype(BF),
        negid=(NEG * np.eye(128, dtype=np.float32)).astype(BF),
        negblock=np.repeat(-np.eye(32, dtype=np.float32), 128, axis=1),
        resetm=_resetm(),
        onesbf=np.ones((128, 128), dtype=BF),
        constv=np.tile(np.array([[LN_EPS, 1.0, -PI]], np.float32), (128, 1)),
    )


def make_in_maps(inputs, n_cores=8):
    x = np.asarray(inputs["L0_attn_hidden_states"])

    def TP(a):
        return np.ascontiguousarray(np.asarray(a).T)

    def SLAB(W):
        WT = np.asarray(W, np.float32).T
        O = WT.shape[1]
        return np.ascontiguousarray(
            WT.reshape(32, 128, O // 128, 128).transpose(2, 1, 0, 3).reshape(O // 128, 128, D))

    consts = make_consts()
    shared = {
        "wqt": SLAB(inputs["L0_attn_Wq"]), "wkt": SLAB(inputs["L0_attn_Wk"]),
        "wvt": SLAB(inputs["L0_attn_Wv"]), "wgt": SLAB(inputs["L0_attn_Wg"]),
        "wot": SLAB(inputs["L0_attn_Wo"]),
        "wint": np.ascontiguousarray(np.asarray(inputs["L0_attn_Win"], np.float32).T
                                     .reshape(32, 128, H).transpose(1, 0, 2).reshape(128, ND * H)),
        "dtb": np.ascontiguousarray(np.asarray(inputs["L0_attn_dt_bias"], np.float32).reshape(H, 1)),
        "alog": np.ascontiguousarray(np.asarray(inputs["L0_attn_A_log_bias"], np.float32).reshape(H, 1)),
        "gnw": np.ascontiguousarray(np.asarray(inputs["L0_attn_gnorm_w"], np.float32).reshape(HD, 1)),
        "wq_b": np.ascontiguousarray(np.asarray(inputs["L0_ffn_bias"], np.float32).reshape(1, D)),
    }
    shared.update(consts)
    in_maps = []
    for c in range(n_cores):
        b, half = c // 2, c % 2
        m = dict(shared)
        m["xT"] = np.ascontiguousarray(x[b, half * T:(half + 1) * T, :].T)
        ks = slice(512 * c, 512 * (c + 1))
        m["wq_qs"] = np.ascontiguousarray(np.asarray(inputs["L0_ffn_W_q"], np.int32)[ks].reshape(4, 128, D))
        m["wq_ms"] = np.ascontiguousarray(np.asarray(inputs["L0_ffn_mask"], np.int32)[ks].reshape(4, 128, D))
        scs = np.asarray(inputs["L0_ffn_scales"], np.float32).reshape(D, 32)[ks].reshape(4, 128, 32)
        zrs = np.asarray(inputs["L0_ffn_zeros"], np.float32).reshape(D, 32)[ks].reshape(4, 128, 32)
        s2s = np.asarray(inputs["L0_ffn_scale2"], np.float32).reshape(D)[ks].reshape(4, 128)
        m["wq_fs"] = np.ascontiguousarray(scs.transpose(1, 0, 2).reshape(128, 128))
        m["wq_zs"] = np.ascontiguousarray(zrs.transpose(1, 0, 2).reshape(128, 128))
        m["wq_s2s"] = np.ascontiguousarray(s2s.T)
        posv = half * T + np.arange(T, dtype=np.float64)
        inv = 1.0 / 10000.0 ** (np.arange(0, HD, 2, dtype=np.float64) / HD)
        ang = posv[:, None] * inv
        m["cos_d"] = np.ascontiguousarray(np.tile(np.cos(ang).T, (2, 1)).astype(BF))
        m["sin_d"] = np.ascontiguousarray(np.tile(np.sin(ang).T, (2, 1)).astype(BF))
        m["ccol"] = np.full((128, 1), float(half), np.float32)
        in_maps.append(m)
    return in_maps


def kernel(**inputs):
    x = np.asarray(inputs["L0_attn_hidden_states"])
    B, L, _ = x.shape
    n_cores = 8
    nc = build_program([[0, 1], [2, 3], [4, 5], [6, 7]])
    in_maps = make_in_maps(inputs, n_cores)
    res = run_bass_kernel_spmd(nc, in_maps, core_ids=list(range(n_cores)))
    outp = np.empty((B, L, D), np.float32)
    for c in range(n_cores):
        b, half = c // 2, c % 2
        outp[b, half * T:(half + 1) * T, :] = res.results[c]["out"].T
    return outp
